# version 17
# baseline (speedup 1.0000x reference)
import numpy as np
import concourse.bass as bass
import concourse.mybir as mybir
from concourse.bass_utils import run_bass_kernel_spmd

F32 = mybir.dt.float32
BF16 = mybir.dt.bfloat16
I32 = mybir.dt.int32
AF = mybir.ActivationFunctionType
ALU = mybir.AluOpType

NCORE = 8
D = 4096
NH = 32
IC = 86
TPC = 1024
HF = 512
SW = 640
CH = [(0, 43), (43, 86)]
NCH = len(CH)

def _chof(i):
    for k, (a, b) in enumerate(CH):
        if i < b:
            return k

def _sblocks(tc):
    return list(range(max(0, tc - 2), min(31, tc + 2) + 1))

_SPLIT = ("w2in", "w1in", "xin", "cvin", "cvout", "dvin", "dvout",
          "r1st", "x2st", "gsl", "usl", "gbl", "ubl", "dbl", "dld",
          "r2st", "outst", "r1ld", "r2ld")
SEM_NAMES = tuple(n + p for n in _SPLIT for p in ("0", "1")) + (
    "mskin", "w2c", "w1c", "xc", "atmm", "atev", "atst", "cca",
    "cvop", "dvop", "ccg", "ccu", "ccd", "sml", "smlop", "ab",
    "y", "gel", "sq", "res", "stats",
    "m1", "v1", "sd1", "a1", "bcB", "x2n",
    "x2l", "xagmm", "xaumm", "xagev", "xauev",
    "gmm", "umm", "lrgev", "lruev", "sttG", "sttU", "sS", "sH",
    "hdamm", "hdaev", "dmm", "dlr", "lrdev", "r2p", "r2ev", "sqd",
    "dstats", "m2", "v2", "sd2", "a2", "bcC", "bccp", "outn", "dv1",
)


def build_program(shared=True):
    nc = bass.Bass("TRN2", target_bir_lowering=False, debug=False)

    xi = nc.dram_tensor("xi", [D, TPC], F32, kind="ExternalInput").ap()
    w1u = nc.dram_tensor("w1u", [NH, 128, 512], F32, kind="ExternalInput").ap()
    w2tu = nc.dram_tensor("w2tu", [NH, 128, 1024], F32, kind="ExternalInput").ap()
    msku = nc.dram_tensor("msku", [128, 4 * SW], F32, kind="ExternalInput").ap()
    b2h = nc.dram_tensor("b2h", [128, NH], F32, kind="ExternalInput").ap()
    gqh = nc.dram_tensor("gqh", [128, IC * 512], I32, kind="ExternalInput").ap()
    uqh = nc.dram_tensor("uqh", [128, IC * 512], I32, kind="ExternalInput").ap()
    dqh = nc.dram_tensor("dqh", [128, IC * 512], I32, kind="ExternalInput").ap()
    gsc = nc.dram_tensor("gsc", [128, IC], F32, kind="ExternalInput").ap()
    usc = nc.dram_tensor("usc", [128, IC], F32, kind="ExternalInput").ap()
    dsc = nc.dram_tensor("dsc", [128, NH], F32, kind="ExternalInput").ap()
    gAh = nc.dram_tensor("gAh", [128, NH * 32], F32, kind="ExternalInput").ap()
    uAh = nc.dram_tensor("uAh", [128, NH * 32], F32, kind="ExternalInput").ap()
    dAh = nc.dram_tensor("dAh", [128, IC * 32], F32, kind="ExternalInput").ap()
    gBh = nc.dram_tensor("gBh", [32, 11008], F32, kind="ExternalInput").ap()
    uBh = nc.dram_tensor("uBh", [32, 11008], F32, kind="ExternalInput").ap()
    dBh = nc.dram_tensor("dBh", [32, D], F32, kind="ExternalInput").ap()

    of = nc.dram_tensor("of", [D, TPC], F32, kind="ExternalOutput").ap()

    adsp = "Shared" if shared else "Local"
    caB = nc.dram_tensor("caB", [128, 4 * SW], BF16)
    gaB = nc.dram_tensor("gaB", [NCORE * 128, 4 * SW], BF16, addr_space=adsp)
    cgs, cus, cds, ggs, gus, gds = [], [], [], [], [], []
    for k, (a, b) in enumerate(CH):
        L = b - a
        cgs.append(nc.dram_tensor("cg%d" % k, [128, L * 512], BF16))
        cus.append(nc.dram_tensor("cu%d" % k, [128, L * 512], BF16))
        cds.append(nc.dram_tensor("cd%d" % k, [128, L * 512], BF16))
        ggs.append(nc.dram_tensor("gg%d" % k, [NCORE * 128, L * 512], BF16, addr_space=adsp))
        gus.append(nc.dram_tensor("gu%d" % k, [NCORE * 128, L * 512], BF16, addr_space=adsp))
        gds.append(nc.dram_tensor("gd%d" % k, [NCORE * 128, L * 512], BF16, addr_space=adsp))
    gB_own = nc.dram_tensor("gB_own", [32, 11008], BF16)
    uB_own = nc.dram_tensor("uB_own", [32, 11008], BF16)
    dB_own = nc.dram_tensor("dB_own", [32, D], BF16)
    r1d = nc.dram_tensor("r1d", [128, 2 * NH * HF], F32)
    x2d = nc.dram_tensor("x2d", [128, 2 * NH * HF], BF16)
    r2d = nc.dram_tensor("r2d", [NH, 128, TPC], F32)

    ggv = [t.ap().rearrange("(r p) f -> r p f", p=128) for t in ggs]
    guv = [t.ap().rearrange("(r p) f -> r p f", p=128) for t in gus]
    gdv = [t.ap().rearrange("(r p) f -> r p f", p=128) for t in gds]
    gav = gaB.ap().rearrange("(r p) f -> r p f", p=128)
    xiv = xi.rearrange("(c p) n -> c p n", p=128)
    r1v = r1d.ap()
    x2v = x2d.ap()
    r2v = r2d.ap()

    import contextlib
    stack = contextlib.ExitStack()
    sem = {n: stack.enter_context(nc.semaphore(n)) for n in SEM_NAMES}

    pstack = contextlib.ExitStack()
    def _sb(*a, **k):
        return pstack.enter_context(nc.sbuf_tensor(*a, **k))
    def _ps(*a, **k):
        return pstack.enter_context(nc.psum_tensor(*a, **k))
    if True:
        bigbuf = _sb("bigbuf", [128, NH * TPC], BF16)
        ones_col = _sb("ones_col", [128, 1], F32)
        ones_row = _sb("ones_row", [1, 128], F32)
        b2sb = _sb("b2sb", [128, NH], F32)
        s3g = _sb("s3g", [128, IC], F32)
        s3u = _sb("s3u", [128, IC], F32)
        s3d = _sb("s3d", [128, NH], F32)
        gAb = _sb("gAb", [128, NH * 32], BF16)
        uAb = _sb("uAb", [128, NH * 32], BF16)
        dAb = _sb("dAb", [128, IC * 32], BF16)
        rows0 = _sb("rows0", [128, 512], F32)
        rows1 = _sb("rows1", [128, 512], F32)
        rows2 = _sb("rows2", [128, 512], F32)
        pb0 = _ps("pb0", [128, 512], F32)
        pb1 = _ps("pb1", [128, 512], F32)
        pb2 = _ps("pb2", [128, 512], F32)
        pb3 = _ps("pb3", [128, 512], F32)
        pb4 = _ps("pb4", [128, 512], F32)
        pb5 = _ps("pb5", [128, 512], F32)
        pb6 = _ps("pb6", [128, 512], F32)
        pb7 = _ps("pb7", [128, 512], F32)
        pb = [pb0, pb1, pb2, pb3, pb4, pb5, pb6, pb7]

        def xbf_t(s):
            return bigbuf[:, s * TPC:(s + 1) * TPC]

        def x2sb_t(hc):
            return bigbuf[:, hc * HF:(hc + 1) * HF]

        cvb = []
        for k, (a, b) in enumerate(CH):
            for srcm, dsts, gouts, cc in ((gqh, cgs, ggs, "ccg"), (uqh, cus, gus, "ccu")):
                i0 = a
                while i0 < b:
                    bsz = min(4, b - i0)
                    fire = (dsts[k], gouts[k], cc) if i0 + bsz >= b else None
                    cvb.append((srcm, dsts[k], i0, i0 - a, bsz, fire))
                    i0 += bsz
        cvop_end = {}
        cv_fires = []
        kk = 0
        c0 = c1 = 0
        for bidx, (_, _, _, _, bsz, fire) in enumerate(cvb):
            kk += bsz
            cvop_end[bidx] = kk
            if bidx % 2 == 0:
                c0 += 1
            else:
                c1 += 1
            if fire is not None:
                cv_fires.append((bidx, c0, c1) + fire)

        astack = contextlib.ExitStack()
        def _sa(*a, **k):
            return astack.enter_context(nc.sbuf_tensor(*a, **k))
        w2stg = _sa("w2stg", [128, 2 * 1024], F32)
        w2bfb = _sa("w2bfb", [128, 2 * 1024], BF16)
        w1stg = _sa("w1stg", [128, 2 * 512], F32)
        w1bfb = _sa("w1bfb", [128, 2 * 512], BF16)
        mskb = _sa("mskb", [128, 4 * SW], F32)
        castg = _sa("castg", [128, 4 * SW], BF16)
        x32a = _sa("x32a", [128, 2 * TPC], F32)
        qcv = _sa("qcv", [128, 8 * 512], I32)
        qbf = _sa("qbf", [128, 8 * 512], BF16)
        smf = _sa("smf", [32, 1024], F32)
        smb = _sa("smb", [32, 1024], BF16)
        tmpf = _sa("tmpf", [128, 512], F32)
        with nc.Block() as blk:
            @blk.sync
            def _(sync):
                sync.dma_start(out=mskb[:], in_=msku[:, :]).then_inc(sem["mskin"], 16)
                for dc in range(NH):
                    par = dc % 2
                    if dc >= 2:
                        sync.wait_ge(sem["w2c"], dc - 1)
                    sync.dma_start(out=w2stg[:, par * 1024:(par + 1) * 1024],
                                   in_=w2tu[dc]).then_inc(sem["w2in%d" % par], 16)
                    if dc >= 2:
                        sync.wait_ge(sem["w1c"], dc - 1)
                    sync.dma_start(out=w1stg[:, par * 512:(par + 1) * 512],
                                   in_=w1u[dc]).then_inc(sem["w1in%d" % par], 16)
                    if dc >= 2:
                        sync.wait_ge(sem["xc"], dc - 1)
                    sync.dma_start(out=x32a[:, par * TPC:(par + 1) * TPC],
                                   in_=xiv[dc]).then_inc(sem["xin%d" % par], 16)
                sync.wait_ge(sem["atev"], 1)
                sync.dma_start(out=caB.ap()[:, :], in_=castg[:]).then_inc(sem["atst"], 16)

            @blk.scalar
            def _(scalar):
                nsml = [0]
                nop = [0]
                sml_steps = []

                def sdma(out, in_):
                    def f():
                        scalar.dma_start(out=out, in_=in_).then_inc(sem["sml"], 16)
                        nsml[0] += 16
                    sml_steps.append(f)

                def swait():
                    def f():
                        scalar.wait_ge(sem["sml"], nsml[0])
                    sml_steps.append(f)

                def sop(out, in_, scale=1.0):
                    def f():
                        scalar.activation(out=out, in_=in_, func=AF.Copy,
                                          scale=scale).then_inc(sem["smlop"], 1)
                        nop[0] += 1
                        scalar.wait_ge(sem["smlop"], nop[0])
                    sml_steps.append(f)

                sdma(b2sb[:], b2h[:, :])
                sdma(s3g[:], gsc[:, :])
                sdma(s3u[:], usc[:, :])
                sdma(s3d[:], dsc[:, :]); swait()
                sop(s3g[:], s3g[:], 0.005)
                sop(s3u[:], s3u[:], 0.005)
                sop(s3d[:], s3d[:], 0.005)
                for srcA, dstA, tot in ((gAh, gAb, NH * 32), (uAh, uAb, NH * 32), (dAh, dAb, IC * 32)):
                    for j0 in range(0, tot, 512):
                        ln = min(512, tot - j0)
                        sdma(tmpf[:, 0:ln], srcA[:, j0:j0 + ln]); swait()
                        sop(dstA[:, j0:j0 + ln], tmpf[:, 0:ln])
                for srcB, dstB, tot in ((gBh, gB_own, 11008), (uBh, uB_own, 11008), (dBh, dB_own, D)):
                    for j0 in range(0, tot, 1024):
                        ln = min(1024, tot - j0)
                        sdma(smf[:, 0:ln], srcB[:, j0:j0 + ln]); swait()
                        sop(smb[:, 0:ln], smf[:, 0:ln], 0.02)
                        sdma(dstB.ap()[:, j0:j0 + ln], smb[:, 0:ln])

                nb = [0, 0]
                nrel = [0]
                cv_steps = []

                def conv_in(bidx):
                    if bidx >= len(cvb):
                        return
                    grp = bidx % 2
                    srcm, _, i0, _, bsz, _ = cvb[bidx]
                    scalar.dma_start(out=qcv[:, grp * 4 * 512: grp * 4 * 512 + bsz * 512],
                                     in_=srcm[:, i0 * 512: i0 * 512 + bsz * 512]).then_inc(sem["cvin%d" % grp], 16)

                def conv_relay():
                    bidx = nrel[0]
                    if bidx >= len(cvb):
                        return
                    _, dstt, _, i0l, bsz, _ = cvb[bidx]
                    grp = bidx % 2
                    scalar.wait_ge(sem["cvop"], cvop_end[bidx])
                    conv_in(bidx + 2)
                    scalar.dma_start(out=dstt.ap()[:, i0l * 512: i0l * 512 + bsz * 512],
                                     in_=qbf[:, grp * 4 * 512: grp * 4 * 512 + bsz * 512]).then_inc(sem["cvout%d" % grp], 16)
                    nb[grp] += 1
                    nrel[0] += 1
                for _ in cvb:
                    cv_steps.append(conv_relay)

                conv_in(0)
                conv_in(1)

                def run(steps, count):
                    for _ in range(count):
                        if steps:
                            steps.pop(0)()

                for dc in range(NH):
                    par = dc % 2
                    scalar.wait_ge(sem["xin%d" % par], 16 * (dc // 2 + 1))
                    scalar.activation(out=xbf_t(dc),
                                      in_=x32a[:, par * TPC:(par + 1) * TPC],
                                      func=AF.Copy).then_inc(sem["xc"], 1)
                    run(cv_steps, 2)
                run(cv_steps, len(cv_steps))
                run(sml_steps, len(sml_steps))
                scalar.wait_ge(sem["sml"], nsml[0])

            @blk.tensor
            def _(tensor):
                for dc in range(NH):
                    par = dc % 2
                    tensor.wait_ge(sem["w2c"], dc + 1)
                    tensor.wait_ge(sem["w1c"], dc + 1)
                    for j in range(4):
                        lhsT = w1bfb[:, par * 512 + j * 128: par * 512 + (j + 1) * 128]
                        tensor.matmul(pb[j][:], lhsT,
                                      w2bfb[:, par * 1024 + j * 128: par * 1024 + j * 128 + 512],
                                      start=(dc == 0), stop=(dc == NH - 1))
                        inst = tensor.matmul(pb[4 + j][:, 0:128], lhsT,
                                             w2bfb[:, par * 1024 + j * 128 + 512: par * 1024 + (j + 1) * 128 + 512],
                                             start=(dc == 0), stop=(dc == NH - 1))
                    inst.then_inc(sem["atmm"], 1)

            @blk.vector
            def _(vector):
                vector.memset(ones_col[:], 1.0)
                vector.memset(ones_row[:], 1.0)
                kcv = [0]

                def conv_ops():
                    bidx = kcv[0]
                    if bidx >= len(cvb):
                        return
                    grp = bidx % 2
                    vector.wait_ge(sem["cvin%d" % grp], 16 * (bidx // 2 + 1))
                    if bidx >= 2:
                        vector.wait_ge(sem["cvout%d" % grp], 16 * (bidx // 2))
                    _, _, _, _, bsz, _ = cvb[bidx]
                    vector.tensor_scalar(out=qbf[:, grp * 4 * 512: grp * 4 * 512 + bsz * 512],
                                         in0=qcv[:, grp * 4 * 512: grp * 4 * 512 + bsz * 512],
                                         scalar1=-7.5, scalar2=None,
                                         op0=ALU.add).then_inc(sem["cvop"], bsz)
                    kcv[0] += 1

                for dc in range(NH):
                    par = dc % 2
                    vector.wait_ge(sem["w2in%d" % par], 16 * (dc // 2 + 1))
                    if dc >= 2:
                        vector.wait_ge(sem["atmm"], dc - 1)
                    vector.tensor_copy(out=w2bfb[:, par * 1024:(par + 1) * 1024],
                                       in_=w2stg[:, par * 1024:(par + 1) * 1024]).then_inc(sem["w2c"], 1)
                    vector.wait_ge(sem["w1in%d" % par], 16 * (dc // 2 + 1))
                    vector.tensor_copy(out=w1bfb[:, par * 512:(par + 1) * 512],
                                       in_=w1stg[:, par * 512:(par + 1) * 512]).then_inc(sem["w1c"], 1)
                    conv_ops()
                    conv_ops()
                while kcv[0] < len(cvb):
                    conv_ops()
                vector.wait_ge(sem["atmm"], NH)
                vector.wait_ge(sem["mskin"], 16)
                for j in range(4):
                    vector.tensor_tensor(out=castg[:, j * SW: j * SW + 512],
                                         in0=pb[j][:],
                                         in1=mskb[:, j * SW: j * SW + 512],
                                         op=ALU.mult)
                    inst = vector.tensor_tensor(out=castg[:, j * SW + 512:(j + 1) * SW],
                                                in0=pb[4 + j][:, 0:128],
                                                in1=mskb[:, j * SW + 512:(j + 1) * SW],
                                                op=ALU.mult)
                inst.then_inc(sem["atev"], 1)

            @blk.gpsimd
            def _(g):
                _, c0, c1, dstt, goutt, cc = cv_fires[0]
                g.wait_ge(sem["cvout0"], 16 * c0)
                g.wait_ge(sem["cvout1"], 16 * c1)
                g.collective_compute("AllGather", ALU.bypass,
                                     replica_groups=[list(range(NCORE))],
                                     ins=[dstt.ap().opt()], outs=[goutt.ap().opt()]).then_inc(sem[cc], 1)
                g.wait_ge(sem["atst"], 16)
                g.collective_compute("AllGather", ALU.bypass,
                                     replica_groups=[list(range(NCORE))],
                                     ins=[caB.ap().opt()], outs=[gaB.ap().opt()]).then_inc(sem["cca"], 1)

        astack.close()

        bstack = contextlib.ExitStack()
        def _sbk(*a, **k):
            return bstack.enter_context(nc.sbuf_tensor(*a, **k))
        attnblk = _sbk("attnblk", [128, NH * SW], BF16)
        dqcv = _sbk("dqcv", [128, 8 * 512], I32)
        dqbf = _sbk("dqbf", [128, 8 * 512], BF16)
        Gf = _sbk("Gf", [128, 2 * HF], F32)
        SQb = _sbk("SQb", [128, 2 * HF], F32)
        Rr = _sbk("Rr", [128, 2 * HF], F32)
        r1ldb = _sbk("r1ldb", [128, 8 * HF], F32)
        x2stg = _sbk("x2stg", [128, 2 * HF], BF16)

        dvb = []
        for k, (a, b) in enumerate(CH):
            i0 = a
            while i0 < b:
                bsz = min(4, b - i0)
                dvb.append((cds[k], gds[k], i0, i0 - a, bsz, k if i0 + bsz >= b else None))
                i0 += bsz
        dvop_end = {}
        dv_fires = []
        kk = 0
        d0 = d1 = 0
        for bidx, (dstt, gdt, _, _, bsz, firek) in enumerate(dvb):
            kk += bsz
            dvop_end[bidx] = kk
            if bidx % 2 == 0:
                d0 += 1
            else:
                d1 += 1
            if firek is not None:
                dv_fires.append((d0, d1, dstt, gdt))

        with nc.Block() as blk:
            @blk.sync
            def _(sync):
                sync.wait_ge(sem["cca"], 1)
                for r in range(NCORE):
                    sync.dma_start(out=attnblk[:, r * 4 * SW:(r + 1) * 4 * SW],
                                   in_=gav[r]).then_inc(sem["ab"], 16)

                nrel = [0]

                def dconv_in(bidx):
                    if bidx >= len(dvb):
                        return
                    grp = bidx % 2
                    _, _, i0, _, bsz, _ = dvb[bidx]
                    sync.dma_start(out=dqcv[:, grp * 4 * 512: grp * 4 * 512 + bsz * 512],
                                   in_=dqh[:, i0 * 512: i0 * 512 + bsz * 512]).then_inc(sem["dvin%d" % grp], 16)

                dconv_in(0)
                dconv_in(1)
                for bidx, (dstt, gdt, _, i0l, bsz, firek) in enumerate(dvb):
                    grp = bidx % 2
                    sync.wait_ge(sem["dvop"], dvop_end[bidx])
                    dconv_in(bidx + 2)
                    sync.dma_start(out=dstt.ap()[:, i0l * 512: i0l * 512 + bsz * 512],
                                   in_=dqbf[:, grp * 4 * 512: grp * 4 * 512 + bsz * 512]).then_inc(sem["dvout%d" % grp], 16)

            @blk.gpsimd
            def _(g):
                for _, c0, c1, dstt, goutt, cc in cv_fires[1:]:
                    g.wait_ge(sem["cvout0"], 16 * c0)
                    g.wait_ge(sem["cvout1"], 16 * c1)
                    g.collective_compute("AllGather", ALU.bypass,
                                         replica_groups=[list(range(NCORE))],
                                         ins=[dstt.ap().opt()], outs=[goutt.ap().opt()]).then_inc(sem[cc], 1)
                for d0, d1, dstt, gdt in dv_fires:
                    g.wait_ge(sem["dvout0"], 16 * d0)
                    g.wait_ge(sem["dvout1"], 16 * d1)
                    g.collective_compute("AllGather", ALU.bypass,
                                         replica_groups=[list(range(NCORE))],
                                         ins=[dstt.ap().opt()], outs=[gdt.ap().opt()]).then_inc(sem["ccd"], 1)
                for half in range(2):
                    for tc in range(NH):
                        k = half * NH + tc
                        par = k % 2
                        if tc % 4 == 0:
                            beta = half * (NH // 4) + tc // 4
                            if beta == 0:
                                g.wait_ge(sem["r1st0"], 16 * NH)
                                g.wait_ge(sem["r1st1"], 16 * NH)
                            if beta >= 2:
                                g.wait_ge(sem["x2n"], half * NH + 4 * (tc // 4 - 1))
                            g.dma_start(out=r1ldb[:, (beta % 2) * 4 * HF:(beta % 2 + 1) * 4 * HF],
                                        in_=r1v[:, half * NH * HF + tc * HF: half * NH * HF + (tc + 4) * HF]).then_inc(sem["r1ld%d" % (beta % 2)], 16)
                        if half == 1:
                            g.wait_ge(sem["x2n"], k + 1)
                            g.dma_start(out=x2v[:, half * NH * HF + tc * HF: half * NH * HF + (tc + 1) * HF],
                                        in_=x2stg[:, par * HF:(par + 1) * HF]).then_inc(sem["x2st%d" % (k % 2)], 16)

            @blk.tensor
            def _(tensor):
                def stats_mms(j):
                    halfj = j % 2
                    parj = j % 2
                    first = (j < 2)
                    last = (j >= 2 * NH - 2)
                    tensor.wait_ge(sem["res"], j + 1)
                    tensor.matmul(pb[2 + 2 * halfj][0:1, :], ones_col[:],
                                  Rr[:, parj * HF:(parj + 1) * HF],
                                  start=first, stop=last)
                    tensor.wait_ge(sem["sq"], j + 1)
                    tensor.matmul(pb[3 + 2 * halfj][0:1, :], ones_col[:],
                                  SQb[:, parj * HF:(parj + 1) * HF],
                                  start=first, stop=last).then_inc(sem["stats"], 1)

                tensor.wait_ge(sem["ab"], 16 * NCORE)
                for tc in range(NH):
                    sbl = _sblocks(tc)
                    for half in range(2):
                        j = 2 * tc + half
                        par = j % 2
                        if j >= 2:
                            tensor.wait_ge(sem["gel"], j - 1)
                        for ksb, bs in enumerate(sbl):
                            off = bs * SW + (tc - (bs - 2)) * 128
                            inst = tensor.matmul(
                                pb[par][:],
                                attnblk[:, off: off + 128],
                                xbf_t(bs)[:, half * HF:(half + 1) * HF],
                                start=(ksb == 0), stop=(ksb == len(sbl) - 1))
                        inst.then_inc(sem["y"], 1)
                        if j >= 2:
                            stats_mms(j - 2)
                stats_mms(2 * NH - 2)
                stats_mms(2 * NH - 1)
                for half in range(2):
                    tensor.wait_ge(sem["a1"], 2 * (half + 1))
                    tensor.matmul(pb[6][:], ones_row[:], rows1[0:1, :], start=True, stop=True)
                    tensor.matmul(pb[7][:], ones_row[:], rows0[0:1, :],
                                  start=True, stop=True).then_inc(sem["bcB"], 1)
                    tensor.wait_ge(sem["x2n"], (half + 1) * NH)

            @blk.scalar
            def _(scalar):
                for j in range(2 * NH):
                    tc = j // 2
                    par = j % 2
                    scalar.wait_ge(sem["y"], j + 1)
                    if j >= 2:
                        scalar.wait_ge(sem["res"], j - 1)
                    scalar.activation(out=Gf[:, par * HF:(par + 1) * HF], in_=pb[par][:],
                                      func=AF.Gelu, bias=b2sb[:, tc:tc + 1],
                                      scale=1.0).then_inc(sem["gel"], 1)
                    scalar.wait_ge(sem["res"], j + 1)
                    if j >= 2:
                        scalar.wait_ge(sem["stats"], j - 1)
                    scalar.activation(out=SQb[:, par * HF:(par + 1) * HF],
                                      in_=Rr[:, par * HF:(par + 1) * HF],
                                      func=AF.Square).then_inc(sem["sq"], 1)
                    half = j % 2
                    scalar.dma_start(out=r1v[:, half * NH * HF + tc * HF: half * NH * HF + (tc + 1) * HF],
                                     in_=Rr[:, par * HF:(par + 1) * HF]).then_inc(sem["r1st%d" % (j % 2)], 16)
                for half in range(2):
                    scalar.wait_ge(sem["v1"], 2 * (half + 1))
                    scalar.activation(out=rows2[0:1, :], in_=rows1[0:1, :],
                                      func=AF.Sqrt).then_inc(sem["sd1"], 1)

            @blk.vector
            def _(vector):
                ndv1 = [0]
                kdv = [0]

                def dconv_ops(count):
                    for _ in range(count):
                        bidx = kdv[0]
                        if bidx >= len(dvb):
                            return
                        grp = bidx % 2
                        vector.wait_ge(sem["dvin%d" % grp], 16 * (bidx // 2 + 1))
                        if bidx >= 2:
                            vector.wait_ge(sem["dvout%d" % grp], 16 * (bidx // 2))
                        _, _, _, _, bsz, _ = dvb[bidx]
                        vector.tensor_scalar(out=dqbf[:, grp * 4 * 512: grp * 4 * 512 + bsz * 512],
                                             in0=dqcv[:, grp * 4 * 512: grp * 4 * 512 + bsz * 512],
                                             scalar1=-7.5, scalar2=None,
                                             op0=ALU.add).then_inc(sem["dvop"], bsz)
                        kdv[0] += 1

                for j in range(2 * NH):
                    tc, half = j // 2, j % 2
                    par = j % 2
                    vector.wait_ge(sem["gel"], j + 1)
                    if j >= 2:
                        vector.wait_ge(sem["sq"], j - 1)
                        vector.wait_ge(sem["stats"], j - 1)
                        vector.wait_ge(sem["r1st%d" % (j % 2)], 16 * (j // 2))
                    vector.tensor_tensor(out=Rr[:, par * HF:(par + 1) * HF],
                                         in0=Gf[:, par * HF:(par + 1) * HF],
                                         in1=xbf_t(tc)[:, half * HF:(half + 1) * HF],
                                         op=ALU.add).then_inc(sem["res"], 1)
                    dconv_ops(1)
                dconv_ops(len(dvb) - kdv[0])
                for half in range(2):
                    vector.wait_ge(sem["stats"], 2 * NH)
                    vector.tensor_scalar(out=rows0[0:1, :], in0=pb[2 + 2 * half][0:1, :],
                                         scalar1=1.0 / D, scalar2=None, op0=ALU.mult)
                    vector.tensor_scalar(out=rows1[0:1, :], in0=pb[3 + 2 * half][0:1, :],
                                         scalar1=1.0 / D, scalar2=None, op0=ALU.mult).then_inc(sem["m1"], 1)
                    vector.wait_ge(sem["m1"], half + 1)
                    vector.tensor_tensor(out=rows2[0:1, :], in0=rows0[0:1, :],
                                         in1=rows0[0:1, :], op=ALU.mult).then_inc(sem["v1"], 1)
                    vector.wait_ge(sem["v1"], 2 * half + 1)
                    vector.scalar_tensor_tensor(out=rows1[0:1, :], in0=rows1[0:1, :],
                                                scalar=1e-5, in1=rows2[0:1, :],
                                                op0=ALU.add, op1=ALU.subtract).then_inc(sem["v1"], 1)
                    vector.wait_ge(sem["sd1"], half + 1)
                    vector.reciprocal(out=rows1[0:1, :], in_=rows2[0:1, :]).then_inc(sem["a1"], 1)
                    vector.wait_ge(sem["a1"], 2 * half + 1)
                    vector.scalar_tensor_tensor(out=rows0[0:1, :], in0=rows0[0:1, :],
                                                scalar=-1.0, in1=rows1[0:1, :],
                                                op0=ALU.mult, op1=ALU.mult).then_inc(sem["a1"], 1)
                    vector.wait_ge(sem["a1"], 2 * half + 2)
                    for tc in range(NH):
                        k = half * NH + tc
                        par = k % 2
                        vector.wait_ge(sem["bcB"], half + 1)
                        beta = half * (NH // 4) + tc // 4
                        vector.wait_ge(sem["r1ld%d" % (beta % 2)], 16 * (beta // 2 + 1))
                        if half == 1 and k >= 34:
                            vector.wait_ge(sem["x2st%d" % (k % 2)], 16 * ((k - 32) // 2))
                        ndv1[0] += 1
                        vector.tensor_tensor(out=Gf[:, par * HF:(par + 1) * HF],
                                             in0=r1ldb[:, (beta % 2) * 4 * HF + (tc % 4) * HF:
                                                      (beta % 2) * 4 * HF + (tc % 4 + 1) * HF],
                                             in1=pb[6][:], op=ALU.mult).then_inc(sem["dv1"], 1)
                        vector.wait_ge(sem["dv1"], ndv1[0])
                        dst_x2 = x2sb_t(tc) if half == 0 else x2stg[:, par * HF:(par + 1) * HF]
                        vector.tensor_tensor(out=dst_x2,
                                             in0=Gf[:, par * HF:(par + 1) * HF],
                                             in1=pb[7][:], op=ALU.add).then_inc(sem["x2n"], 1)

        bstack.close()

        DSLABS = []
        for k, (a, b) in enumerate(CH):
            i0 = a
            while i0 < b:
                ssz = min(4, b - i0)
                DSLABS.append((i0, ssz, k))
                i0 += ssz
        nslab = len(DSLABS)

        cstack = contextlib.ExitStack()
        def _sc(*a, **k):
            return cstack.enter_context(nc.sbuf_tensor(*a, **k))
        hrest = _sc("hrest", [128, (IC - NH) * HF], BF16)
        gslab = _sc("gslab", [128, 2 * D], BF16)
        uslab = _sc("uslab", [128, 2 * D], BF16)
        dtile = _sc("dtile", [128, 2 * 4 * HF], BF16)
        gBt = _sc("gBt", [32, 2 * 128], BF16)
        uBt = _sc("uBt", [32, 2 * 128], BF16)
        dBt = _sc("dBt", [32, 2 * 128], BF16)
        xag = _sc("xag", [32, HF], BF16)
        xau = _sc("xau", [32, HF], BF16)
        hdabuf = _sc("hdab", [32, HF], BF16)
        LRg = _sc("LRg", [128, 2 * HF], F32)
        LRu = _sc("LRu", [128, 2 * HF], F32)
        Gb = _sc("Gb", [128, 2 * HF], F32)
        Ub = _sc("Ub", [128, 2 * HF], F32)
        Sb = _sc("Sb", [128, 2 * HF], F32)
        r2stg = _sc("r2stg", [128, 2 * HF], F32)
        outstg = _sc("outstg", [128, 2 * HF], F32)
        with nc.Block() as blk:
            def h_tile(i):
                if i < NH:
                    return bigbuf[:, NH * HF + i * HF: NH * HF + (i + 1) * HF]
                return hrest[:, (i - NH) * HF:(i - NH + 1) * HF]

            @blk.sync
            def _(sync):
                dmm_end_of_slab = {}
                sync.wait_ge(sem["x2st0"], 16 * (NH // 2))
                sync.wait_ge(sem["x2st1"], 16 * (NH // 2))
                for half in range(2):
                    if half == 1:
                        sync.wait_ge(sem["gmm"], IC)
                        sync.wait_ge(sem["umm"], IC)
                        sync.wait_ge(sem["xagmm"], 1)
                        sync.wait_ge(sem["xaumm"], 1)
                        sync.wait_ge(sem["r2ev"], NH)
                        sync.dma_start(out=bigbuf[:, 0:NH * HF],
                                       in_=x2v[:, NH * HF:2 * NH * HF]).then_inc(sem["x2l"], 16)
                    for i in range(IC):
                        kch = _chof(i)
                        iloc = i - CH[kch][0]
                        ii = half * IC + i
                        par = ii % 2
                        if half == 0 and iloc == 0:
                            sync.wait_ge(sem["ccg"], kch + 1)
                            sync.wait_ge(sem["ccu"], kch + 1)
                        if ii >= 2:
                            sync.wait_ge(sem["gmm"], ii - 1)
                        for r in range(NCORE):
                            sync.dma_start(out=gslab[:, par * D + r * HF: par * D + (r + 1) * HF],
                                           in_=ggv[kch][r][:, iloc * HF:(iloc + 1) * HF]).then_inc(sem["gsl%d" % par], 16)
                        if ii >= 2:
                            sync.wait_ge(sem["umm"], ii - 1)
                        for r in range(NCORE):
                            sync.dma_start(out=uslab[:, par * D + r * HF: par * D + (r + 1) * HF],
                                           in_=guv[kch][r][:, iloc * HF:(iloc + 1) * HF]).then_inc(sem["usl%d" % par], 16)
                        if ii >= 2:
                            sync.wait_ge(sem["gmm"], ii - 1)
                        sync.dma_start(out=gBt[:, par * 128:(par + 1) * 128],
                                       in_=gB_own.ap()[:, i * 128:(i + 1) * 128]).then_inc(sem["gbl%d" % par], 16)
                        if ii >= 2:
                            sync.wait_ge(sem["umm"], ii - 1)
                        sync.dma_start(out=uBt[:, par * 128:(par + 1) * 128],
                                       in_=uB_own.ap()[:, i * 128:(i + 1) * 128]).then_inc(sem["ubl%d" % par], 16)
                    for r in range(NCORE):
                        for si, (i0, ssz, kch) in enumerate(DSLABS):
                            q = half * NCORE * nslab + r * nslab + si
                            if half == 0 and r == 0 and i0 == CH[kch][0]:
                                sync.wait_ge(sem["ccd"], kch + 1)
                            if q >= 2:
                                sync.wait_ge(sem["dmm"], dmm_end_of_slab[q - 2])
                            qp = q % 2
                            sync.dma_start(out=dtile[:, qp * 4 * HF: qp * 4 * HF + ssz * HF],
                                           in_=gdv[kch][r][:, (i0 - CH[kch][0]) * HF: (i0 - CH[kch][0]) * HF + ssz * HF]).then_inc(sem["dld%d" % qp], 16)
                            dmm_end_of_slab[q] = half * NCORE * IC + r * IC + i0 + ssz
                        for ol in range(4):
                            oc = 4 * r + ol
                            kk2 = half * NH + oc
                            if kk2 >= 2:
                                sync.wait_ge(sem["dlr"], kk2 - 1)
                            sync.dma_start(out=dBt[:, (kk2 % 2) * 128:(kk2 % 2 + 1) * 128],
                                           in_=dB_own.ap()[:, oc * 128:(oc + 1) * 128]).then_inc(sem["dbl%d" % (kk2 % 2)], 16)

            @blk.tensor
            def _(tensor):
                for half in range(2):
                    if half == 0:
                        tensor.wait_ge(sem["x2n"], NH)
                    else:
                        tensor.wait_ge(sem["x2l"], 16)
                    if half == 1:
                        tensor.wait_ge(sem["r2p"], NH)
                        tensor.wait_ge(sem["m2"], 1)
                        tensor.wait_ge(sem["bccp"], 1)
                    for hc in range(NH):
                        inst = tensor.matmul(pb[6][0:32, :], gAb[:, hc * 32:(hc + 1) * 32],
                                             x2sb_t(hc), start=(hc == 0), stop=(hc == NH - 1))
                    inst.then_inc(sem["xagmm"], 1)
                    tensor.wait_ge(sem["xagev"], half + 1)
                    for hc in range(NH):
                        inst = tensor.matmul(pb[6][0:32, :], uAb[:, hc * 32:(hc + 1) * 32],
                                             x2sb_t(hc), start=(hc == 0), stop=(hc == NH - 1))
                    inst.then_inc(sem["xaumm"], 1)
                    tensor.wait_ge(sem["xauev"], half + 1)

                    for i in range(IC):
                        ii = half * IC + i
                        par = ii % 2
                        tensor.wait_ge(sem["gsl%d" % par], 16 * NCORE * (ii // 2 + 1))
                        if ii >= 2:
                            tensor.wait_ge(sem["sttG"], ii - 1)
                        for hc in range(NH):
                            r, hl = hc // 4, hc % 4
                            tensor.matmul(pb[par][:],
                                          gslab[:, par * D + r * HF + hl * 128:
                                                par * D + r * HF + hl * 128 + 128],
                                          x2sb_t(hc), start=(hc == 0), stop=(hc == NH - 1))
                        tensor.wait_ge(sem["gbl%d" % par], 16 * (ii // 2 + 1))
                        tensor.wait_ge(sem["lrgev"], ii)
                        tensor.matmul(pb[4][:], gBt[:, par * 128:(par + 1) * 128],
                                      xag[:, :], start=True, stop=True).then_inc(sem["gmm"], 1)
                        tensor.wait_ge(sem["usl%d" % par], 16 * NCORE * (ii // 2 + 1))
                        if ii >= 2:
                            tensor.wait_ge(sem["sttU"], ii - 1)
                        for hc in range(NH):
                            r, hl = hc // 4, hc % 4
                            tensor.matmul(pb[2 + par][:],
                                          uslab[:, par * D + r * HF + hl * 128:
                                                par * D + r * HF + hl * 128 + 128],
                                          x2sb_t(hc), start=(hc == 0), stop=(hc == NH - 1))
                        tensor.wait_ge(sem["ubl%d" % par], 16 * (ii // 2 + 1))
                        tensor.wait_ge(sem["lruev"], ii)
                        tensor.matmul(pb[5][:], uBt[:, par * 128:(par + 1) * 128],
                                      xau[:, :], start=True, stop=True).then_inc(sem["umm"], 1)
                        if i >= 2:
                            ih = i - 2
                            tensor.wait_ge(sem["sH"], half * IC + ih + 1)
                            tensor.matmul(pb[6][0:32, :], dAb[:, ih * 32:(ih + 1) * 32],
                                          h_tile(ih), start=(ih == 0), stop=False)
                    for ih in (IC - 2, IC - 1):
                        tensor.wait_ge(sem["sH"], half * IC + ih + 1)
                        inst = tensor.matmul(pb[6][0:32, :], dAb[:, ih * 32:(ih + 1) * 32],
                                             h_tile(ih), start=False, stop=(ih == IC - 1))
                    inst.then_inc(sem["hdamm"], 1)

                    tensor.wait_ge(sem["hdaev"], half + 1)
                    tensor.wait_ge(sem["lrgev"], (half + 1) * IC)
                    tensor.wait_ge(sem["lruev"], (half + 1) * IC)
                    for r in range(NCORE):
                        if r >= 1:
                            tensor.wait_ge(sem["r2p"], half * NH + 4 * r)
                        for si, (i0, ssz, kch) in enumerate(DSLABS):
                            q = half * NCORE * nslab + r * nslab + si
                            qp = q % 2
                            tensor.wait_ge(sem["dld%d" % qp], 16 * (q // 2 + 1))
                            for c in range(ssz):
                                i = i0 + c
                                for ol in range(4):
                                    inst = tensor.matmul(pb[ol][:],
                                                         dtile[:, qp * 4 * HF + c * HF + ol * 128:
                                                               qp * 4 * HF + c * HF + ol * 128 + 128],
                                                         h_tile(i), start=(i == 0), stop=(i == IC - 1))
                                inst.then_inc(sem["dmm"], 1)
                        for ol in range(4):
                            oc = 4 * r + ol
                            kk2 = half * NH + oc
                            tensor.wait_ge(sem["dbl%d" % (kk2 % 2)], 16 * (kk2 // 2 + 1))
                            tensor.wait_ge(sem["lrdev"], kk2 - 1 if kk2 >= 2 else 0)
                            tensor.matmul(pb[4 + (kk2 % 2)][:], dBt[:, (kk2 % 2) * 128:(kk2 % 2 + 1) * 128],
                                          hdabuf[:, :], start=True, stop=True).then_inc(sem["dlr"], 1)
                        for ol in range(4):
                            oc = 4 * r + ol
                            kk2 = half * NH + oc
                            parp = kk2 % 2
                            first = (oc == 0)
                            tensor.wait_ge(sem["r2ev"], kk2 + 1)
                            tensor.matmul(pb[6][0:1, :], ones_col[:],
                                          r2stg[:, parp * HF:(parp + 1) * HF],
                                          start=first, stop=(oc == NH - 1))
                            tensor.wait_ge(sem["sqd"], kk2 + 1)
                            tensor.matmul(pb[7][0:1, :], ones_col[:],
                                          Sb[:, parp * HF:(parp + 1) * HF],
                                          start=first, stop=(oc == NH - 1)).then_inc(sem["dstats"], 1)
                    tensor.wait_ge(sem["a2"], 2 * (half + 1))
                    tensor.wait_ge(sem["lrdev"], (half + 1) * NH)
                    tensor.matmul(pb[4][:], ones_row[:], rows1[0:1, :], start=True, stop=True)
                    tensor.matmul(pb[5][:], ones_row[:], rows0[0:1, :],
                                  start=True, stop=True).then_inc(sem["bcC"], 1)

            @blk.scalar
            def _(scalar):
                for half in range(2):
                    for i in range(IC):
                        ii = half * IC + i
                        par = ii % 2
                        scalar.wait_ge(sem["gmm"], ii + 1)
                        if ii >= 2:
                            scalar.wait_ge(sem["sttG"], ii - 1)
                        scalar.activation(out=LRg[:, par * HF:(par + 1) * HF], in_=pb[4][:],
                                          func=AF.Copy).then_inc(sem["lrgev"], 1)
                        scalar.wait_ge(sem["umm"], ii + 1)
                        if ii >= 2:
                            scalar.wait_ge(sem["sttU"], ii - 1)
                        scalar.activation(out=LRu[:, par * HF:(par + 1) * HF], in_=pb[5][:],
                                          func=AF.Copy).then_inc(sem["lruev"], 1)
                        scalar.wait_ge(sem["sttG"], ii + 1)
                        if ii >= 2:
                            scalar.wait_ge(sem["sH"], ii - 1)
                        scalar.activation(out=Sb[:, par * HF:(par + 1) * HF],
                                          in_=Gb[:, par * HF:(par + 1) * HF],
                                          func=AF.Silu).then_inc(sem["sS"], 1)
                    for oc in range(NH):
                        kk2 = half * NH + oc
                        parp = kk2 % 2
                        scalar.wait_ge(sem["dlr"], kk2 + 1)
                        if kk2 >= 2:
                            scalar.wait_ge(sem["r2p"], kk2 - 1)
                        scalar.activation(out=LRu[:, parp * HF:(parp + 1) * HF],
                                          in_=pb[4 + (kk2 % 2)][:],
                                          func=AF.Copy).then_inc(sem["lrdev"], 1)
                        scalar.wait_ge(sem["r2ev"], kk2 + 1)
                        if kk2 >= 2:
                            scalar.wait_ge(sem["dstats"], kk2 - 1)
                        scalar.activation(out=Sb[:, parp * HF:(parp + 1) * HF],
                                          in_=r2stg[:, parp * HF:(parp + 1) * HF],
                                          func=AF.Square).then_inc(sem["sqd"], 1)
                    scalar.wait_ge(sem["v2"], 2 * (half + 1))
                    scalar.activation(out=rows2[0:1, :], in_=rows1[0:1, :],
                                      func=AF.Sqrt).then_inc(sem["sd2"], 1)

            @blk.vector
            def _(vector):
                ndv1c = [0]
                norm_steps = []
                for half in range(2):
                    vector.wait_ge(sem["xagmm"], half + 1)
                    if half == 1:
                        vector.wait_ge(sem["gmm"], IC)
                    vector.tensor_copy(out=xag[:, :], in_=pb[6][0:32, :]).then_inc(sem["xagev"], 1)
                    vector.wait_ge(sem["xaumm"], half + 1)
                    if half == 1:
                        vector.wait_ge(sem["umm"], IC)
                    vector.tensor_copy(out=xau[:, :], in_=pb[6][0:32, :]).then_inc(sem["xauev"], 1)
                    for i in range(IC):
                        ii = half * IC + i
                        par = ii % 2
                        vector.wait_ge(sem["gmm"], ii + 1)
                        vector.wait_ge(sem["lrgev"], ii + 1)
                        if ii >= 2:
                            vector.wait_ge(sem["sS"], ii - 1)
                        vector.scalar_tensor_tensor(out=Gb[:, par * HF:(par + 1) * HF],
                                                    in0=pb[par][:], scalar=s3g[:, i:i + 1],
                                                    in1=LRg[:, par * HF:(par + 1) * HF],
                                                    op0=ALU.mult, op1=ALU.add).then_inc(sem["sttG"], 1)
                        vector.wait_ge(sem["umm"], ii + 1)
                        vector.wait_ge(sem["lruev"], ii + 1)
                        if ii >= 2:
                            vector.wait_ge(sem["sH"], ii - 1)
                        vector.scalar_tensor_tensor(out=Ub[:, par * HF:(par + 1) * HF],
                                                    in0=pb[2 + par][:], scalar=s3u[:, i:i + 1],
                                                    in1=LRu[:, par * HF:(par + 1) * HF],
                                                    op0=ALU.mult, op1=ALU.add).then_inc(sem["sttU"], 1)
                        vector.wait_ge(sem["sS"], ii + 1)
                        vector.wait_ge(sem["sttU"], ii + 1)
                        if half == 1 and i == 0:
                            vector.wait_ge(sem["dmm"], NCORE * IC)
                        vector.tensor_tensor(out=h_tile(i),
                                             in0=Sb[:, par * HF:(par + 1) * HF],
                                             in1=Ub[:, par * HF:(par + 1) * HF],
                                             op=ALU.mult).then_inc(sem["sH"], 1)
                        if half == 1 and norm_steps:
                            norm_steps.pop(0)()
                    vector.wait_ge(sem["hdamm"], half + 1)
                    if half == 1:
                        vector.wait_ge(sem["dlr"], NH)
                    vector.tensor_copy(out=hdabuf[:, :], in_=pb[6][0:32, :]).then_inc(sem["hdaev"], 1)
                    vector.wait_ge(sem["sS"], (half + 1) * IC)
                    for r in range(NCORE):
                        for ol in range(4):
                            oc = 4 * r + ol
                            kk2 = half * NH + oc
                            parp = kk2 % 2
                            vector.wait_ge(sem["dmm"], half * NCORE * IC + (r + 1) * IC)
                            vector.wait_ge(sem["lrdev"], kk2 + 1)
                            vector.scalar_tensor_tensor(out=Gb[:, parp * HF:(parp + 1) * HF],
                                                        in0=pb[ol][:], scalar=s3d[:, oc:oc + 1],
                                                        in1=LRu[:, parp * HF:(parp + 1) * HF],
                                                        op0=ALU.mult, op1=ALU.add).then_inc(sem["r2p"], 1)
                            vector.wait_ge(sem["r2p"], kk2 + 1)
                            if kk2 >= 2:
                                vector.wait_ge(sem["sqd"], kk2 - 1)
                                vector.wait_ge(sem["dstats"], kk2 - 1)
                                vector.wait_ge(sem["r2st%d" % (kk2 % 2)], 16 * (kk2 // 2))
                            vector.tensor_tensor(out=r2stg[:, parp * HF:(parp + 1) * HF],
                                                 in0=Gb[:, parp * HF:(parp + 1) * HF],
                                                 in1=x2sb_t(oc), op=ALU.add).then_inc(sem["r2ev"], 1)
                    vector.wait_ge(sem["dstats"], (half + 1) * NH)
                    vector.tensor_scalar(out=rows0[0:1, :], in0=pb[6][0:1, :],
                                         scalar1=1.0 / D, scalar2=None, op0=ALU.mult)
                    vector.tensor_scalar(out=rows1[0:1, :], in0=pb[7][0:1, :],
                                         scalar1=1.0 / D, scalar2=None, op0=ALU.mult).then_inc(sem["m2"], 1)
                    vector.wait_ge(sem["m2"], half + 1)
                    vector.tensor_tensor(out=rows2[0:1, :], in0=rows0[0:1, :],
                                         in1=rows0[0:1, :], op=ALU.mult).then_inc(sem["v2"], 1)
                    vector.wait_ge(sem["v2"], 2 * half + 1)
                    vector.scalar_tensor_tensor(out=rows1[0:1, :], in0=rows1[0:1, :],
                                                scalar=1e-5, in1=rows2[0:1, :],
                                                op0=ALU.add, op1=ALU.subtract).then_inc(sem["v2"], 1)
                    vector.wait_ge(sem["sd2"], half + 1)
                    vector.reciprocal(out=rows1[0:1, :], in_=rows2[0:1, :]).then_inc(sem["a2"], 1)
                    vector.wait_ge(sem["a2"], 2 * half + 1)
                    vector.scalar_tensor_tensor(out=rows0[0:1, :], in0=rows0[0:1, :],
                                                scalar=-1.0, in1=rows1[0:1, :],
                                                op0=ALU.mult, op1=ALU.mult).then_inc(sem["a2"], 1)
                    vector.wait_ge(sem["bcC"], half + 1)
                    vector.wait_ge(sem["r2st0"], 16 * (NH // 2) * (half + 1))
                    vector.wait_ge(sem["r2st1"], 16 * (NH // 2) * (half + 1))
                    vector.tensor_copy(out=r2stg[:, 0:HF], in_=pb[4][:])
                    vector.tensor_copy(out=r2stg[:, HF:2 * HF], in_=pb[5][:]).then_inc(sem["bccp"], 1)
                    vector.wait_ge(sem["bccp"], half + 1)
                    def mk_norm(half_, oc):
                        def f():
                            kk2 = half_ * NH + oc
                            parp = kk2 % 2
                            vector.wait_ge(sem["r2ld%d" % (kk2 % 2)], 16 * (kk2 // 2 + 1))
                            if kk2 >= 2:
                                vector.wait_ge(sem["outst%d" % (kk2 % 2)], 16 * (kk2 // 2))
                            ndv1c[0] += 1
                            vector.tensor_tensor(out=outstg[:, parp * HF:(parp + 1) * HF],
                                                 in0=outstg[:, parp * HF:(parp + 1) * HF],
                                                 in1=r2stg[:, 0:HF], op=ALU.mult).then_inc(sem["dv1"], 1)
                            vector.wait_ge(sem["dv1"], 64 + ndv1c[0])
                            vector.tensor_tensor(out=outstg[:, parp * HF:(parp + 1) * HF],
                                                 in0=outstg[:, parp * HF:(parp + 1) * HF],
                                                 in1=r2stg[:, HF:2 * HF], op=ALU.add).then_inc(sem["outn"], 1)
                        return f
                    if half == 0:
                        for oc in range(NH):
                            norm_steps.append(mk_norm(0, oc))
                    else:
                        for oc in range(NH):
                            mk_norm(1, oc)()

            @blk.gpsimd
            def _(g):
                ofv = of.rearrange("(c p) n -> c p n", p=128)
                for half in range(2):
                    for oc in range(NH):
                        kk2 = half * NH + oc
                        parp = kk2 % 2
                        g.wait_ge(sem["r2ev"], kk2 + 1)
                        g.dma_start(out=r2v[oc, :, half * HF:(half + 1) * HF],
                                    in_=r2stg[:, parp * HF:(parp + 1) * HF]).then_inc(sem["r2st%d" % (kk2 % 2)], 16)
                    for oc in range(NH):
                        kk2 = half * NH + oc
                        parp = kk2 % 2
                        if oc == 0:
                            g.wait_ge(sem["r2st0"], 16 * (NH // 2) * (half + 1))
                            g.wait_ge(sem["r2st1"], 16 * (NH // 2) * (half + 1))
                        if kk2 >= 2:
                            g.wait_ge(sem["outn"], kk2 - 1)
                            g.wait_ge(sem["outst%d" % (kk2 % 2)], 16 * (kk2 // 2))
                        g.dma_start(out=outstg[:, parp * HF:(parp + 1) * HF],
                                    in_=r2v[oc, :, half * HF:(half + 1) * HF]).then_inc(sem["r2ld%d" % (kk2 % 2)], 16)
                        g.wait_ge(sem["outn"], kk2 + 1)
                        g.dma_start(out=ofv[oc][:, half * HF:(half + 1) * HF],
                                    in_=outstg[:, parp * HF:(parp + 1) * HF]).then_inc(sem["outst%d" % (kk2 % 2)], 16)
                g.wait_ge(sem["outst0"], 16 * NH)
                g.wait_ge(sem["outst1"], 16 * NH)

        cstack.close()
    pstack.close()
    stack.close()
    nc.finalize()
    return nc


def _prep_inputs(inputs):
    A = {k: np.asarray(v) for k, v in inputs.items()}
    x = A["L0_attn_x"].astype(np.float32, copy=False)
    w1 = A["L0_attn_w1"]; w2 = A["L0_attn_w2"]; b2 = A["L0_attn_b2"]
    mask = A["L0_attn_sparse_mask"]
    x_fm = np.ascontiguousarray(x.reshape(NTOK_, D).T)
    w2T = np.ascontiguousarray(w2.T)
    b2hh = np.ascontiguousarray(b2.reshape(NH, 128).T)

    gq = A["L0_ffn_gate_q"]; uq = A["L0_ffn_up_q"]; dq = A["L0_ffn_down_q"]
    gqb = gq.T.reshape(NH, 128, IC, 128)
    uqb = uq.T.reshape(NH, 128, IC, 128)
    dqb = dq.T.reshape(IC, 128, NH, 128)
    gsch = np.ascontiguousarray(A["L0_ffn_gate_scale"].reshape(IC, 128).T)
    usch = np.ascontiguousarray(A["L0_ffn_up_scale"].reshape(IC, 128).T)
    dsch = np.ascontiguousarray(A["L0_ffn_down_scale"].reshape(NH, 128).T)
    gAhh = np.ascontiguousarray(
        A["L0_ffn_gate_A"].reshape(NH, 128, 32).transpose(1, 0, 2)).reshape(128, NH * 32)
    uAhh = np.ascontiguousarray(
        A["L0_ffn_up_A"].reshape(NH, 128, 32).transpose(1, 0, 2)).reshape(128, NH * 32)
    dAhh = np.ascontiguousarray(
        A["L0_ffn_down_A"].reshape(IC, 128, 32).transpose(1, 0, 2)).reshape(128, IC * 32)
    gBhh = np.ascontiguousarray(A["L0_ffn_gate_B"])
    uBhh = np.ascontiguousarray(A["L0_ffn_up_B"])
    dBhh = np.ascontiguousarray(A["L0_ffn_down_B"])

    in_maps = []
    for c in range(NCORE):
        w1uc = np.ascontiguousarray(
            w1[:, c * 512:(c + 1) * 512].reshape(NH, 128, 512))
        lo = 4 * c - 2
        w2uc = np.zeros((D, 1024), dtype=np.float32)
        blo, bhi = max(0, lo), min(NH, 4 * c + 6)
        w2uc[:, (blo - lo) * 128:(bhi - lo) * 128] = w2T[:, blo * 128:bhi * 128]
        w2uc = np.ascontiguousarray(w2uc.reshape(NH, 128, 1024))
        mskc = np.zeros((128, 4 * SW), dtype=np.float32)
        for j in range(4):
            bs = 4 * c + j
            slo = bs - 2
            sblo, sbhi = max(0, slo), min(NH, bs + 3)
            mskc[:, j * SW + (sblo - slo) * 128: j * SW + (sbhi - slo) * 128] = \
                mask[bs * 128:(bs + 1) * 128, sblo * 128:sbhi * 128]
        in_maps.append({
            "xi": np.ascontiguousarray(x_fm[:, c * TPC:(c + 1) * TPC]),
            "w1u": w1uc,
            "w2tu": w2uc,
            "msku": mskc,
            "b2h": b2hh,
            "gqh": np.ascontiguousarray(gqb[4 * c:4 * c + 4].transpose(1, 2, 0, 3)).reshape(128, IC * 512),
            "uqh": np.ascontiguousarray(uqb[4 * c:4 * c + 4].transpose(1, 2, 0, 3)).reshape(128, IC * 512),
            "dqh": np.ascontiguousarray(dqb.transpose(1, 0, 2, 3)[:, :, 4 * c:4 * c + 4, :]).reshape(128, IC * 512),
            "gsc": gsch, "usc": usch, "dsc": dsch,
            "gAh": gAhh, "uAh": uAhh, "dAh": dAhh,
            "gBh": gBhh, "uBh": uBhh, "dBh": dBhh,
        })
    return in_maps


NTOK_ = 8192


def kernel(**inputs):
    in_maps = _prep_inputs(inputs)
    nc = build_program()
    res = run_bass_kernel_spmd(nc, in_maps, core_ids=list(range(NCORE)))
    out_fm = np.concatenate([res.results[c]["of"] for c in range(NCORE)], axis=1)
    return np.ascontiguousarray(out_fm.T).reshape(2, D, D).astype(np.float32)


# version 18 (final)
# speedup vs baseline: 1.0060x; 1.0060x over previous
import numpy as np
import concourse.bass as bass
import concourse.mybir as mybir
from concourse.bass_utils import run_bass_kernel_spmd

F32 = mybir.dt.float32
BF16 = mybir.dt.bfloat16
I32 = mybir.dt.int32
AF = mybir.ActivationFunctionType
ALU = mybir.AluOpType

NCORE = 8
D = 4096
NH = 32
IC = 86
TPC = 1024
HF = 512
SW = 640
CH = [(0, 43), (43, 86)]
NCH = len(CH)

def _chof(i):
    for k, (a, b) in enumerate(CH):
        if i < b:
            return k

def _sblocks(tc):
    return list(range(max(0, tc - 2), min(31, tc + 2) + 1))

_SPLIT = ("w2in", "w1in", "xin", "cvin", "cvout", "dvin", "dvout",
          "r1st", "x2st", "gsl", "usl", "gbl", "ubl", "dbl", "dld",
          "r2st", "outst", "r1ld", "r2ld")
SEM_NAMES = tuple(n + p for n in _SPLIT for p in ("0", "1")) + (
    "mskin", "w2c", "w1c", "xc", "atmm", "atev", "atst", "cca",
    "cvop", "dvop", "ccg", "ccu", "ccd", "sml", "smlop", "ab",
    "y", "gel", "sq", "res", "stats",
    "m1", "v1", "sd1", "a1", "bcB", "x2n",
    "x2l", "xagmm", "xaumm", "xagev", "xauev",
    "gmm", "umm", "lrgev", "lruev", "sttG", "sttU", "sS", "sH",
    "hdamm", "hdaev", "dmm", "dlr", "lrdev", "r2p", "r2ev", "sqd",
    "dstats", "m2", "v2", "sd2", "a2", "bcC", "bccp", "outn", "dv1",
)


def build_program(shared=True):
    nc = bass.Bass("TRN2", target_bir_lowering=False, debug=False)

    xi = nc.dram_tensor("xi", [D, TPC], F32, kind="ExternalInput").ap()
    w1u = nc.dram_tensor("w1u", [NH, 128, 512], F32, kind="ExternalInput").ap()
    w2tu = nc.dram_tensor("w2tu", [NH, 128, 1024], F32, kind="ExternalInput").ap()
    msku = nc.dram_tensor("msku", [128, 4 * SW], F32, kind="ExternalInput").ap()
    b2h = nc.dram_tensor("b2h", [128, NH], F32, kind="ExternalInput").ap()
    gqh = nc.dram_tensor("gqh", [128, IC * 512], I32, kind="ExternalInput").ap()
    uqh = nc.dram_tensor("uqh", [128, IC * 512], I32, kind="ExternalInput").ap()
    dqh = nc.dram_tensor("dqh", [128, IC * 512], I32, kind="ExternalInput").ap()
    gsc = nc.dram_tensor("gsc", [128, IC], F32, kind="ExternalInput").ap()
    usc = nc.dram_tensor("usc", [128, IC], F32, kind="ExternalInput").ap()
    dsc = nc.dram_tensor("dsc", [128, NH], F32, kind="ExternalInput").ap()
    gAh = nc.dram_tensor("gAh", [128, NH * 32], F32, kind="ExternalInput").ap()
    uAh = nc.dram_tensor("uAh", [128, NH * 32], F32, kind="ExternalInput").ap()
    dAh = nc.dram_tensor("dAh", [128, IC * 32], F32, kind="ExternalInput").ap()
    gBh = nc.dram_tensor("gBh", [32, 11008], F32, kind="ExternalInput").ap()
    uBh = nc.dram_tensor("uBh", [32, 11008], F32, kind="ExternalInput").ap()
    dBh = nc.dram_tensor("dBh", [32, D], F32, kind="ExternalInput").ap()

    of = nc.dram_tensor("of", [D, TPC], F32, kind="ExternalOutput").ap()

    adsp = "Shared" if shared else "Local"
    caB = nc.dram_tensor("caB", [128, 4 * SW], BF16)
    gaB = nc.dram_tensor("gaB", [NCORE * 128, 4 * SW], BF16, addr_space=adsp)
    cgs, cus, cds, ggs, gus, gds = [], [], [], [], [], []
    for k, (a, b) in enumerate(CH):
        L = b - a
        cgs.append(nc.dram_tensor("cg%d" % k, [128, L * 512], BF16))
        cus.append(nc.dram_tensor("cu%d" % k, [128, L * 512], BF16))
        cds.append(nc.dram_tensor("cd%d" % k, [128, L * 512], BF16))
        ggs.append(nc.dram_tensor("gg%d" % k, [NCORE * 128, L * 512], BF16, addr_space=adsp))
        gus.append(nc.dram_tensor("gu%d" % k, [NCORE * 128, L * 512], BF16, addr_space=adsp))
        gds.append(nc.dram_tensor("gd%d" % k, [NCORE * 128, L * 512], BF16, addr_space=adsp))
    gB_own = nc.dram_tensor("gB_own", [32, 11008], BF16)
    uB_own = nc.dram_tensor("uB_own", [32, 11008], BF16)
    dB_own = nc.dram_tensor("dB_own", [32, D], BF16)
    r1d = nc.dram_tensor("r1d", [128, 2 * NH * HF], F32)
    x2d = nc.dram_tensor("x2d", [128, 2 * NH * HF], BF16)
    r2d = nc.dram_tensor("r2d", [NH, 128, TPC], F32)

    ggv = [t.ap().rearrange("(r p) f -> r p f", p=128) for t in ggs]
    guv = [t.ap().rearrange("(r p) f -> r p f", p=128) for t in gus]
    gdv = [t.ap().rearrange("(r p) f -> r p f", p=128) for t in gds]
    gav = gaB.ap().rearrange("(r p) f -> r p f", p=128)
    xiv = xi.rearrange("(c p) n -> c p n", p=128)
    r1v = r1d.ap()
    x2v = x2d.ap()
    r2v = r2d.ap()

    import contextlib
    stack = contextlib.ExitStack()
    sem = {n: stack.enter_context(nc.semaphore(n)) for n in SEM_NAMES}

    pstack = contextlib.ExitStack()
    def _sb(*a, **k):
        return pstack.enter_context(nc.sbuf_tensor(*a, **k))
    def _ps(*a, **k):
        return pstack.enter_context(nc.psum_tensor(*a, **k))
    if True:
        bigbuf = _sb("bigbuf", [128, NH * TPC], BF16)
        ones_col = _sb("ones_col", [128, 1], F32)
        ones_row = _sb("ones_row", [1, 128], F32)
        b2sb = _sb("b2sb", [128, NH], F32)
        s3g = _sb("s3g", [128, IC], F32)
        s3u = _sb("s3u", [128, IC], F32)
        s3d = _sb("s3d", [128, NH], F32)
        gAb = _sb("gAb", [128, NH * 32], BF16)
        uAb = _sb("uAb", [128, NH * 32], BF16)
        dAb = _sb("dAb", [128, IC * 32], BF16)
        rows0 = _sb("rows0", [128, 512], F32)
        rows1 = _sb("rows1", [128, 512], F32)
        rows2 = _sb("rows2", [128, 512], F32)
        pb0 = _ps("pb0", [128, 512], F32)
        pb1 = _ps("pb1", [128, 512], F32)
        pb2 = _ps("pb2", [128, 512], F32)
        pb3 = _ps("pb3", [128, 512], F32)
        pb4 = _ps("pb4", [128, 512], F32)
        pb5 = _ps("pb5", [128, 512], F32)
        pb6 = _ps("pb6", [128, 512], F32)
        pb7 = _ps("pb7", [128, 512], F32)
        pb = [pb0, pb1, pb2, pb3, pb4, pb5, pb6, pb7]

        def xbf_t(s):
            return bigbuf[:, s * TPC:(s + 1) * TPC]

        def x2sb_t(hc):
            return bigbuf[:, hc * HF:(hc + 1) * HF]

        cvb = []
        for k, (a, b) in enumerate(CH):
            for srcm, dsts, gouts, cc in ((gqh, cgs, ggs, "ccg"), (uqh, cus, gus, "ccu")):
                i0 = a
                while i0 < b:
                    bsz = min(4, b - i0)
                    fire = (dsts[k], gouts[k], cc) if i0 + bsz >= b else None
                    cvb.append((srcm, dsts[k], i0, i0 - a, bsz, fire))
                    i0 += bsz
        cvop_end = {}
        cv_fires = []
        kk = 0
        c0 = c1 = 0
        for bidx, (_, _, _, _, bsz, fire) in enumerate(cvb):
            kk += bsz
            cvop_end[bidx] = kk
            if bidx % 2 == 0:
                c0 += 1
            else:
                c1 += 1
            if fire is not None:
                cv_fires.append((bidx, c0, c1) + fire)

        astack = contextlib.ExitStack()
        def _sa(*a, **k):
            return astack.enter_context(nc.sbuf_tensor(*a, **k))
        w2stg = _sa("w2stg", [128, 2 * 1024], F32)
        w2bfb = _sa("w2bfb", [128, 2 * 1024], BF16)
        w1stg = _sa("w1stg", [128, 2 * 512], F32)
        w1bfb = _sa("w1bfb", [128, 2 * 512], BF16)
        mskb = _sa("mskb", [128, 4 * SW], F32)
        castg = _sa("castg", [128, 4 * SW], BF16)
        x32a = _sa("x32a", [128, 2 * TPC], F32)
        qcv = _sa("qcv", [128, 8 * 512], I32)
        qbf = _sa("qbf", [128, 8 * 512], BF16)
        smf = _sa("smf", [32, 1024], F32)
        smb = _sa("smb", [32, 1024], BF16)
        tmpf = _sa("tmpf", [128, 512], F32)
        with nc.Block() as blk:
            @blk.sync
            def _(sync):
                sync.dma_start(out=mskb[:], in_=msku[:, :]).then_inc(sem["mskin"], 16)
                for dc in range(NH):
                    par = dc % 2
                    if dc >= 2:
                        sync.wait_ge(sem["w2c"], dc - 1)
                    sync.dma_start(out=w2stg[:, par * 1024:(par + 1) * 1024],
                                   in_=w2tu[dc]).then_inc(sem["w2in%d" % par], 16)
                    if dc >= 2:
                        sync.wait_ge(sem["w1c"], dc - 1)
                    sync.dma_start(out=w1stg[:, par * 512:(par + 1) * 512],
                                   in_=w1u[dc]).then_inc(sem["w1in%d" % par], 16)
                    if dc >= 2:
                        sync.wait_ge(sem["xc"], dc - 1)
                    sync.dma_start(out=x32a[:, par * TPC:(par + 1) * TPC],
                                   in_=xiv[dc]).then_inc(sem["xin%d" % par], 16)
                sync.wait_ge(sem["atev"], 1)
                sync.dma_start(out=caB.ap()[:, :], in_=castg[:]).then_inc(sem["atst"], 16)

            @blk.scalar
            def _(scalar):
                nsml = [0]
                nop = [0]
                sml_steps = []

                def sdma(out, in_):
                    def f():
                        scalar.dma_start(out=out, in_=in_).then_inc(sem["sml"], 16)
                        nsml[0] += 16
                    sml_steps.append(f)

                def swait():
                    def f():
                        scalar.wait_ge(sem["sml"], nsml[0])
                    sml_steps.append(f)

                def sop(out, in_, scale=1.0):
                    def f():
                        scalar.activation(out=out, in_=in_, func=AF.Copy,
                                          scale=scale).then_inc(sem["smlop"], 1)
                        nop[0] += 1
                        scalar.wait_ge(sem["smlop"], nop[0])
                    sml_steps.append(f)

                sdma(b2sb[:], b2h[:, :])
                sdma(s3g[:], gsc[:, :])
                sdma(s3u[:], usc[:, :])
                sdma(s3d[:], dsc[:, :]); swait()
                sop(s3g[:], s3g[:], 0.005)
                sop(s3u[:], s3u[:], 0.005)
                sop(s3d[:], s3d[:], 0.005)
                for srcA, dstA, tot in ((gAh, gAb, NH * 32), (uAh, uAb, NH * 32), (dAh, dAb, IC * 32)):
                    for j0 in range(0, tot, 512):
                        ln = min(512, tot - j0)
                        sdma(tmpf[:, 0:ln], srcA[:, j0:j0 + ln]); swait()
                        sop(dstA[:, j0:j0 + ln], tmpf[:, 0:ln])
                for srcB, dstB, tot in ((gBh, gB_own, 11008), (uBh, uB_own, 11008), (dBh, dB_own, D)):
                    for j0 in range(0, tot, 1024):
                        ln = min(1024, tot - j0)
                        sdma(smf[:, 0:ln], srcB[:, j0:j0 + ln]); swait()
                        sop(smb[:, 0:ln], smf[:, 0:ln], 0.02)
                        sdma(dstB.ap()[:, j0:j0 + ln], smb[:, 0:ln])

                nb = [0, 0]
                nrel = [0]
                cv_steps = []

                def conv_in(bidx):
                    if bidx >= len(cvb):
                        return
                    grp = bidx % 2
                    srcm, _, i0, _, bsz, _ = cvb[bidx]
                    scalar.dma_start(out=qcv[:, grp * 4 * 512: grp * 4 * 512 + bsz * 512],
                                     in_=srcm[:, i0 * 512: i0 * 512 + bsz * 512]).then_inc(sem["cvin%d" % grp], 16)

                def conv_relay():
                    bidx = nrel[0]
                    if bidx >= len(cvb):
                        return
                    _, dstt, _, i0l, bsz, _ = cvb[bidx]
                    grp = bidx % 2
                    scalar.wait_ge(sem["cvop"], cvop_end[bidx])
                    conv_in(bidx + 2)
                    scalar.dma_start(out=dstt.ap()[:, i0l * 512: i0l * 512 + bsz * 512],
                                     in_=qbf[:, grp * 4 * 512: grp * 4 * 512 + bsz * 512]).then_inc(sem["cvout%d" % grp], 16)
                    nb[grp] += 1
                    nrel[0] += 1
                for _ in cvb:
                    cv_steps.append(conv_relay)

                conv_in(0)
                conv_in(1)

                def run(steps, count):
                    for _ in range(count):
                        if steps:
                            steps.pop(0)()

                for dc in range(NH):
                    par = dc % 2
                    scalar.wait_ge(sem["xin%d" % par], 16 * (dc // 2 + 1))
                    scalar.activation(out=xbf_t(dc),
                                      in_=x32a[:, par * TPC:(par + 1) * TPC],
                                      func=AF.Copy).then_inc(sem["xc"], 1)
                    run(cv_steps, 2)
                run(cv_steps, len(cv_steps))
                run(sml_steps, len(sml_steps))
                scalar.wait_ge(sem["sml"], nsml[0])

            @blk.tensor
            def _(tensor):
                for dc in range(NH):
                    par = dc % 2
                    tensor.wait_ge(sem["w2c"], dc + 1)
                    tensor.wait_ge(sem["w1c"], dc + 1)
                    for j in range(4):
                        lhsT = w1bfb[:, par * 512 + j * 128: par * 512 + (j + 1) * 128]
                        tensor.matmul(pb[j][:], lhsT,
                                      w2bfb[:, par * 1024 + j * 128: par * 1024 + j * 128 + 512],
                                      start=(dc == 0), stop=(dc == NH - 1))
                        inst = tensor.matmul(pb[4 + j][:, 0:128], lhsT,
                                             w2bfb[:, par * 1024 + j * 128 + 512: par * 1024 + (j + 1) * 128 + 512],
                                             start=(dc == 0), stop=(dc == NH - 1))
                    inst.then_inc(sem["atmm"], 1)

            @blk.vector
            def _(vector):
                vector.memset(ones_col[:], 1.0)
                vector.memset(ones_row[:], 1.0)
                kcv = [0]

                def conv_ops():
                    bidx = kcv[0]
                    if bidx >= len(cvb):
                        return
                    grp = bidx % 2
                    vector.wait_ge(sem["cvin%d" % grp], 16 * (bidx // 2 + 1))
                    if bidx >= 2:
                        vector.wait_ge(sem["cvout%d" % grp], 16 * (bidx // 2))
                    _, _, _, _, bsz, _ = cvb[bidx]
                    vector.tensor_scalar(out=qbf[:, grp * 4 * 512: grp * 4 * 512 + bsz * 512],
                                         in0=qcv[:, grp * 4 * 512: grp * 4 * 512 + bsz * 512],
                                         scalar1=-7.5, scalar2=None,
                                         op0=ALU.add).then_inc(sem["cvop"], bsz)
                    kcv[0] += 1

                for dc in range(NH):
                    par = dc % 2
                    vector.wait_ge(sem["w2in%d" % par], 16 * (dc // 2 + 1))
                    if dc >= 2:
                        vector.wait_ge(sem["atmm"], dc - 1)
                    vector.tensor_copy(out=w2bfb[:, par * 1024:(par + 1) * 1024],
                                       in_=w2stg[:, par * 1024:(par + 1) * 1024]).then_inc(sem["w2c"], 1)
                    vector.wait_ge(sem["w1in%d" % par], 16 * (dc // 2 + 1))
                    vector.tensor_copy(out=w1bfb[:, par * 512:(par + 1) * 512],
                                       in_=w1stg[:, par * 512:(par + 1) * 512]).then_inc(sem["w1c"], 1)
                    conv_ops()
                    conv_ops()
                while kcv[0] < len(cvb):
                    conv_ops()
                vector.wait_ge(sem["atmm"], NH)
                vector.wait_ge(sem["mskin"], 16)
                for j in range(4):
                    vector.tensor_tensor(out=castg[:, j * SW: j * SW + 512],
                                         in0=pb[j][:],
                                         in1=mskb[:, j * SW: j * SW + 512],
                                         op=ALU.mult)
                    inst = vector.tensor_tensor(out=castg[:, j * SW + 512:(j + 1) * SW],
                                                in0=pb[4 + j][:, 0:128],
                                                in1=mskb[:, j * SW + 512:(j + 1) * SW],
                                                op=ALU.mult)
                inst.then_inc(sem["atev"], 1)

            @blk.gpsimd
            def _(g):
                _, c0, c1, dstt, goutt, cc = cv_fires[0]
                g.wait_ge(sem["cvout0"], 16 * c0)
                g.wait_ge(sem["cvout1"], 16 * c1)
                g.collective_compute("AllGather", ALU.bypass,
                                     replica_groups=[list(range(NCORE))],
                                     ins=[dstt.ap().opt()], outs=[goutt.ap().opt()]).then_inc(sem[cc], 1)
                g.wait_ge(sem["atst"], 16)
                g.collective_compute("AllGather", ALU.bypass,
                                     replica_groups=[list(range(NCORE))],
                                     ins=[caB.ap().opt()], outs=[gaB.ap().opt()]).then_inc(sem["cca"], 1)

        astack.close()

        bstack = contextlib.ExitStack()
        def _sbk(*a, **k):
            return bstack.enter_context(nc.sbuf_tensor(*a, **k))
        attnblk = _sbk("attnblk", [128, NH * SW], BF16)
        dqcv = _sbk("dqcv", [128, 8 * 512], I32)
        dqbf = _sbk("dqbf", [128, 8 * 512], BF16)
        Gf = _sbk("Gf", [128, 2 * HF], F32)
        SQb = _sbk("SQb", [128, 2 * HF], F32)
        Rr = _sbk("Rr", [128, 2 * HF], F32)
        r1ldb = _sbk("r1ldb", [128, 8 * HF], F32)
        x2stg = _sbk("x2stg", [128, 2 * HF], BF16)

        dvb = []
        for k, (a, b) in enumerate(CH):
            i0 = a
            while i0 < b:
                bsz = min(4, b - i0)
                dvb.append((cds[k], gds[k], i0, i0 - a, bsz, k if i0 + bsz >= b else None))
                i0 += bsz
        dvop_end = {}
        dv_fires = []
        kk = 0
        d0 = d1 = 0
        for bidx, (dstt, gdt, _, _, bsz, firek) in enumerate(dvb):
            kk += bsz
            dvop_end[bidx] = kk
            if bidx % 2 == 0:
                d0 += 1
            else:
                d1 += 1
            if firek is not None:
                dv_fires.append((d0, d1, dstt, gdt))

        with nc.Block() as blk:
            @blk.sync
            def _(sync):
                sync.wait_ge(sem["cca"], 1)
                for r in range(NCORE):
                    sync.dma_start(out=attnblk[:, r * 4 * SW:(r + 1) * 4 * SW],
                                   in_=gav[r]).then_inc(sem["ab"], 16)

                nrel = [0]

                def dconv_in(bidx):
                    if bidx >= len(dvb):
                        return
                    grp = bidx % 2
                    _, _, i0, _, bsz, _ = dvb[bidx]
                    sync.dma_start(out=dqcv[:, grp * 4 * 512: grp * 4 * 512 + bsz * 512],
                                   in_=dqh[:, i0 * 512: i0 * 512 + bsz * 512]).then_inc(sem["dvin%d" % grp], 16)

                dconv_in(0)
                dconv_in(1)
                for bidx, (dstt, gdt, _, i0l, bsz, firek) in enumerate(dvb):
                    grp = bidx % 2
                    sync.wait_ge(sem["dvop"], dvop_end[bidx])
                    dconv_in(bidx + 2)
                    sync.dma_start(out=dstt.ap()[:, i0l * 512: i0l * 512 + bsz * 512],
                                   in_=dqbf[:, grp * 4 * 512: grp * 4 * 512 + bsz * 512]).then_inc(sem["dvout%d" % grp], 16)

            @blk.gpsimd
            def _(g):
                for _, c0, c1, dstt, goutt, cc in cv_fires[1:]:
                    g.wait_ge(sem["cvout0"], 16 * c0)
                    g.wait_ge(sem["cvout1"], 16 * c1)
                    g.collective_compute("AllGather", ALU.bypass,
                                         replica_groups=[list(range(NCORE))],
                                         ins=[dstt.ap().opt()], outs=[goutt.ap().opt()]).then_inc(sem[cc], 1)
                for d0, d1, dstt, gdt in dv_fires:
                    g.wait_ge(sem["dvout0"], 16 * d0)
                    g.wait_ge(sem["dvout1"], 16 * d1)
                    g.collective_compute("AllGather", ALU.bypass,
                                         replica_groups=[list(range(NCORE))],
                                         ins=[dstt.ap().opt()], outs=[gdt.ap().opt()]).then_inc(sem["ccd"], 1)
                for half in range(2):
                    for tc in range(NH):
                        k = half * NH + tc
                        par = k % 2
                        if tc % 4 == 0:
                            beta = half * (NH // 4) + tc // 4
                            if beta == 0:
                                g.wait_ge(sem["r1st0"], 16 * NH)
                                g.wait_ge(sem["r1st1"], 16 * NH)
                            if beta >= 2:
                                g.wait_ge(sem["x2n"], half * NH + 4 * (tc // 4 - 1))
                            g.dma_start(out=r1ldb[:, (beta % 2) * 4 * HF:(beta % 2 + 1) * 4 * HF],
                                        in_=r1v[:, half * NH * HF + tc * HF: half * NH * HF + (tc + 4) * HF]).then_inc(sem["r1ld%d" % (beta % 2)], 16)
                        if half == 1:
                            g.wait_ge(sem["x2n"], k + 1)
                            g.dma_start(out=x2v[:, half * NH * HF + tc * HF: half * NH * HF + (tc + 1) * HF],
                                        in_=x2stg[:, par * HF:(par + 1) * HF]).then_inc(sem["x2st%d" % (k % 2)], 16)

            @blk.tensor
            def _(tensor):
                def stats_mms(j):
                    halfj = j % 2
                    parj = j % 2
                    first = (j < 2)
                    last = (j >= 2 * NH - 2)
                    tensor.wait_ge(sem["res"], j + 1)
                    tensor.matmul(pb[2 + 2 * halfj][0:1, :], ones_col[:],
                                  Rr[:, parj * HF:(parj + 1) * HF],
                                  start=first, stop=last)
                    tensor.wait_ge(sem["sq"], j + 1)
                    tensor.matmul(pb[3 + 2 * halfj][0:1, :], ones_col[:],
                                  SQb[:, parj * HF:(parj + 1) * HF],
                                  start=first, stop=last).then_inc(sem["stats"], 1)

                tensor.wait_ge(sem["ab"], 16 * NCORE)
                for tc in range(NH):
                    sbl = _sblocks(tc)
                    for half in range(2):
                        j = 2 * tc + half
                        par = j % 2
                        if j >= 2:
                            tensor.wait_ge(sem["gel"], j - 1)
                        for ksb, bs in enumerate(sbl):
                            off = bs * SW + (tc - (bs - 2)) * 128
                            inst = tensor.matmul(
                                pb[par][:],
                                attnblk[:, off: off + 128],
                                xbf_t(bs)[:, half * HF:(half + 1) * HF],
                                start=(ksb == 0), stop=(ksb == len(sbl) - 1))
                        inst.then_inc(sem["y"], 1)
                        if j >= 2:
                            stats_mms(j - 2)
                stats_mms(2 * NH - 2)
                stats_mms(2 * NH - 1)
                for half in range(2):
                    tensor.wait_ge(sem["a1"], 2 * (half + 1))
                    tensor.matmul(pb[6][:], ones_row[:], rows1[0:1, :], start=True, stop=True)
                    tensor.matmul(pb[7][:], ones_row[:], rows0[0:1, :],
                                  start=True, stop=True).then_inc(sem["bcB"], 1)
                    tensor.wait_ge(sem["x2n"], (half + 1) * NH)

            @blk.scalar
            def _(scalar):
                for j in range(2 * NH):
                    tc = j // 2
                    par = j % 2
                    scalar.wait_ge(sem["y"], j + 1)
                    if j >= 2:
                        scalar.wait_ge(sem["res"], j - 1)
                    scalar.activation(out=Gf[:, par * HF:(par + 1) * HF], in_=pb[par][:],
                                      func=AF.Gelu, bias=b2sb[:, tc:tc + 1],
                                      scale=1.0).then_inc(sem["gel"], 1)
                    scalar.wait_ge(sem["res"], j + 1)
                    if j >= 2:
                        scalar.wait_ge(sem["stats"], j - 1)
                    scalar.activation(out=SQb[:, par * HF:(par + 1) * HF],
                                      in_=Rr[:, par * HF:(par + 1) * HF],
                                      func=AF.Square).then_inc(sem["sq"], 1)
                    half = j % 2
                    scalar.dma_start(out=r1v[:, half * NH * HF + tc * HF: half * NH * HF + (tc + 1) * HF],
                                     in_=Rr[:, par * HF:(par + 1) * HF]).then_inc(sem["r1st%d" % (j % 2)], 16)
                for half in range(2):
                    scalar.wait_ge(sem["v1"], 2 * (half + 1))
                    scalar.activation(out=rows2[0:1, :], in_=rows1[0:1, :],
                                      func=AF.Sqrt).then_inc(sem["sd1"], 1)

            @blk.vector
            def _(vector):
                ndv1 = [0]
                kdv = [0]
                cv0_tot = sum(1 for b in range(len(cvb)) if b % 2 == 0)
                cv1_tot = len(cvb) - cv0_tot
                vector.wait_ge(sem["cvout0"], 16 * cv0_tot)
                vector.wait_ge(sem["cvout1"], 16 * cv1_tot)

                def dconv_ops(count):
                    for _ in range(count):
                        bidx = kdv[0]
                        if bidx >= len(dvb):
                            return
                        grp = bidx % 2
                        vector.wait_ge(sem["dvin%d" % grp], 16 * (bidx // 2 + 1))
                        if bidx >= 2:
                            vector.wait_ge(sem["dvout%d" % grp], 16 * (bidx // 2))
                        _, _, _, _, bsz, _ = dvb[bidx]
                        vector.tensor_scalar(out=dqbf[:, grp * 4 * 512: grp * 4 * 512 + bsz * 512],
                                             in0=dqcv[:, grp * 4 * 512: grp * 4 * 512 + bsz * 512],
                                             scalar1=-7.5, scalar2=None,
                                             op0=ALU.add).then_inc(sem["dvop"], bsz)
                        kdv[0] += 1

                for j in range(2 * NH):
                    tc, half = j // 2, j % 2
                    par = j % 2
                    vector.wait_ge(sem["gel"], j + 1)
                    if j >= 2:
                        vector.wait_ge(sem["sq"], j - 1)
                        vector.wait_ge(sem["stats"], j - 1)
                        vector.wait_ge(sem["r1st%d" % (j % 2)], 16 * (j // 2))
                    vector.tensor_tensor(out=Rr[:, par * HF:(par + 1) * HF],
                                         in0=Gf[:, par * HF:(par + 1) * HF],
                                         in1=xbf_t(tc)[:, half * HF:(half + 1) * HF],
                                         op=ALU.add).then_inc(sem["res"], 1)
                    dconv_ops(1)
                dconv_ops(len(dvb) - kdv[0])
                for half in range(2):
                    vector.wait_ge(sem["stats"], 2 * NH)
                    vector.tensor_scalar(out=rows0[0:1, :], in0=pb[2 + 2 * half][0:1, :],
                                         scalar1=1.0 / D, scalar2=None, op0=ALU.mult)
                    vector.tensor_scalar(out=rows1[0:1, :], in0=pb[3 + 2 * half][0:1, :],
                                         scalar1=1.0 / D, scalar2=None, op0=ALU.mult).then_inc(sem["m1"], 1)
                    vector.wait_ge(sem["m1"], half + 1)
                    vector.tensor_tensor(out=rows2[0:1, :], in0=rows0[0:1, :],
                                         in1=rows0[0:1, :], op=ALU.mult).then_inc(sem["v1"], 1)
                    vector.wait_ge(sem["v1"], 2 * half + 1)
                    vector.scalar_tensor_tensor(out=rows1[0:1, :], in0=rows1[0:1, :],
                                                scalar=1e-5, in1=rows2[0:1, :],
                                                op0=ALU.add, op1=ALU.subtract).then_inc(sem["v1"], 1)
                    vector.wait_ge(sem["sd1"], half + 1)
                    vector.reciprocal(out=rows1[0:1, :], in_=rows2[0:1, :]).then_inc(sem["a1"], 1)
                    vector.wait_ge(sem["a1"], 2 * half + 1)
                    vector.scalar_tensor_tensor(out=rows0[0:1, :], in0=rows0[0:1, :],
                                                scalar=-1.0, in1=rows1[0:1, :],
                                                op0=ALU.mult, op1=ALU.mult).then_inc(sem["a1"], 1)
                    vector.wait_ge(sem["a1"], 2 * half + 2)
                    for tc in range(NH):
                        k = half * NH + tc
                        par = k % 2
                        vector.wait_ge(sem["bcB"], half + 1)
                        beta = half * (NH // 4) + tc // 4
                        vector.wait_ge(sem["r1ld%d" % (beta % 2)], 16 * (beta // 2 + 1))
                        if half == 1 and k >= 34:
                            vector.wait_ge(sem["x2st%d" % (k % 2)], 16 * ((k - 32) // 2))
                        ndv1[0] += 1
                        vector.tensor_tensor(out=Gf[:, par * HF:(par + 1) * HF],
                                             in0=r1ldb[:, (beta % 2) * 4 * HF + (tc % 4) * HF:
                                                      (beta % 2) * 4 * HF + (tc % 4 + 1) * HF],
                                             in1=pb[6][:], op=ALU.mult).then_inc(sem["dv1"], 1)
                        vector.wait_ge(sem["dv1"], ndv1[0])
                        dst_x2 = x2sb_t(tc) if half == 0 else x2stg[:, par * HF:(par + 1) * HF]
                        vector.tensor_tensor(out=dst_x2,
                                             in0=Gf[:, par * HF:(par + 1) * HF],
                                             in1=pb[7][:], op=ALU.add).then_inc(sem["x2n"], 1)

        bstack.close()

        DSLABS = []
        for k, (a, b) in enumerate(CH):
            i0 = a
            while i0 < b:
                ssz = min(4, b - i0)
                DSLABS.append((i0, ssz, k))
                i0 += ssz
        nslab = len(DSLABS)

        cstack = contextlib.ExitStack()
        def _sc(*a, **k):
            return cstack.enter_context(nc.sbuf_tensor(*a, **k))
        hrest = _sc("hrest", [128, (IC - NH) * HF], BF16)
        gslab = _sc("gslab", [128, 2 * D], BF16)
        uslab = _sc("uslab", [128, 2 * D], BF16)
        dtile = _sc("dtile", [128, 2 * 4 * HF], BF16)
        gBt = _sc("gBt", [32, 2 * 128], BF16)
        uBt = _sc("uBt", [32, 2 * 128], BF16)
        dBt = _sc("dBt", [32, 2 * 128], BF16)
        xag = _sc("xag", [32, HF], BF16)
        xau = _sc("xau", [32, HF], BF16)
        hdabuf = _sc("hdab", [32, HF], BF16)
        LRg = _sc("LRg", [128, 2 * HF], F32)
        LRu = _sc("LRu", [128, 2 * HF], F32)
        Gb = _sc("Gb", [128, 2 * HF], F32)
        Ub = _sc("Ub", [128, 2 * HF], F32)
        Sb = _sc("Sb", [128, 2 * HF], F32)
        r2stg = _sc("r2stg", [128, 2 * HF], F32)
        outstg = _sc("outstg", [128, 2 * HF], F32)
        with nc.Block() as blk:
            def h_tile(i):
                if i < NH:
                    return bigbuf[:, NH * HF + i * HF: NH * HF + (i + 1) * HF]
                return hrest[:, (i - NH) * HF:(i - NH + 1) * HF]

            @blk.sync
            def _(sync):
                dmm_end_of_slab = {}
                sync.wait_ge(sem["x2st0"], 16 * (NH // 2))
                sync.wait_ge(sem["x2st1"], 16 * (NH // 2))
                for half in range(2):
                    if half == 1:
                        sync.wait_ge(sem["gmm"], IC)
                        sync.wait_ge(sem["umm"], IC)
                        sync.wait_ge(sem["xagmm"], 1)
                        sync.wait_ge(sem["xaumm"], 1)
                        sync.wait_ge(sem["r2ev"], NH)
                        sync.dma_start(out=bigbuf[:, 0:NH * HF],
                                       in_=x2v[:, NH * HF:2 * NH * HF]).then_inc(sem["x2l"], 16)
                    for i in range(IC):
                        kch = _chof(i)
                        iloc = i - CH[kch][0]
                        ii = half * IC + i
                        par = ii % 2
                        if half == 0 and iloc == 0:
                            sync.wait_ge(sem["ccg"], kch + 1)
                            sync.wait_ge(sem["ccu"], kch + 1)
                        if ii >= 2:
                            sync.wait_ge(sem["gmm"], ii - 1)
                        for r in range(NCORE):
                            sync.dma_start(out=gslab[:, par * D + r * HF: par * D + (r + 1) * HF],
                                           in_=ggv[kch][r][:, iloc * HF:(iloc + 1) * HF]).then_inc(sem["gsl%d" % par], 16)
                        if ii >= 2:
                            sync.wait_ge(sem["umm"], ii - 1)
                        for r in range(NCORE):
                            sync.dma_start(out=uslab[:, par * D + r * HF: par * D + (r + 1) * HF],
                                           in_=guv[kch][r][:, iloc * HF:(iloc + 1) * HF]).then_inc(sem["usl%d" % par], 16)
                        if ii >= 2:
                            sync.wait_ge(sem["gmm"], ii - 1)
                        sync.dma_start(out=gBt[:, par * 128:(par + 1) * 128],
                                       in_=gB_own.ap()[:, i * 128:(i + 1) * 128]).then_inc(sem["gbl%d" % par], 16)
                        if ii >= 2:
                            sync.wait_ge(sem["umm"], ii - 1)
                        sync.dma_start(out=uBt[:, par * 128:(par + 1) * 128],
                                       in_=uB_own.ap()[:, i * 128:(i + 1) * 128]).then_inc(sem["ubl%d" % par], 16)
                    for r in range(NCORE):
                        for si, (i0, ssz, kch) in enumerate(DSLABS):
                            q = half * NCORE * nslab + r * nslab + si
                            if half == 0 and r == 0 and i0 == CH[kch][0]:
                                sync.wait_ge(sem["ccd"], kch + 1)
                            if q >= 2:
                                sync.wait_ge(sem["dmm"], dmm_end_of_slab[q - 2])
                            qp = q % 2
                            sync.dma_start(out=dtile[:, qp * 4 * HF: qp * 4 * HF + ssz * HF],
                                           in_=gdv[kch][r][:, (i0 - CH[kch][0]) * HF: (i0 - CH[kch][0]) * HF + ssz * HF]).then_inc(sem["dld%d" % qp], 16)
                            dmm_end_of_slab[q] = half * NCORE * IC + r * IC + i0 + ssz
                        for ol in range(4):
                            oc = 4 * r + ol
                            kk2 = half * NH + oc
                            if kk2 >= 2:
                                sync.wait_ge(sem["dlr"], kk2 - 1)
                            sync.dma_start(out=dBt[:, (kk2 % 2) * 128:(kk2 % 2 + 1) * 128],
                                           in_=dB_own.ap()[:, oc * 128:(oc + 1) * 128]).then_inc(sem["dbl%d" % (kk2 % 2)], 16)

            @blk.tensor
            def _(tensor):
                for half in range(2):
                    if half == 0:
                        tensor.wait_ge(sem["x2n"], NH)
                    else:
                        tensor.wait_ge(sem["x2l"], 16)
                    if half == 1:
                        tensor.wait_ge(sem["r2p"], NH)
                        tensor.wait_ge(sem["m2"], 1)
                        tensor.wait_ge(sem["bccp"], 1)
                    for hc in range(NH):
                        inst = tensor.matmul(pb[6][0:32, :], gAb[:, hc * 32:(hc + 1) * 32],
                                             x2sb_t(hc), start=(hc == 0), stop=(hc == NH - 1))
                    inst.then_inc(sem["xagmm"], 1)
                    tensor.wait_ge(sem["xagev"], half + 1)
                    for hc in range(NH):
                        inst = tensor.matmul(pb[6][0:32, :], uAb[:, hc * 32:(hc + 1) * 32],
                                             x2sb_t(hc), start=(hc == 0), stop=(hc == NH - 1))
                    inst.then_inc(sem["xaumm"], 1)
                    tensor.wait_ge(sem["xauev"], half + 1)

                    for i in range(IC):
                        ii = half * IC + i
                        par = ii % 2
                        tensor.wait_ge(sem["gsl%d" % par], 16 * NCORE * (ii // 2 + 1))
                        if ii >= 2:
                            tensor.wait_ge(sem["sttG"], ii - 1)
                        for hc in range(NH):
                            r, hl = hc // 4, hc % 4
                            tensor.matmul(pb[par][:],
                                          gslab[:, par * D + r * HF + hl * 128:
                                                par * D + r * HF + hl * 128 + 128],
                                          x2sb_t(hc), start=(hc == 0), stop=(hc == NH - 1))
                        tensor.wait_ge(sem["gbl%d" % par], 16 * (ii // 2 + 1))
                        tensor.wait_ge(sem["lrgev"], ii)
                        tensor.matmul(pb[4][:], gBt[:, par * 128:(par + 1) * 128],
                                      xag[:, :], start=True, stop=True).then_inc(sem["gmm"], 1)
                        tensor.wait_ge(sem["usl%d" % par], 16 * NCORE * (ii // 2 + 1))
                        if ii >= 2:
                            tensor.wait_ge(sem["sttU"], ii - 1)
                        for hc in range(NH):
                            r, hl = hc // 4, hc % 4
                            tensor.matmul(pb[2 + par][:],
                                          uslab[:, par * D + r * HF + hl * 128:
                                                par * D + r * HF + hl * 128 + 128],
                                          x2sb_t(hc), start=(hc == 0), stop=(hc == NH - 1))
                        tensor.wait_ge(sem["ubl%d" % par], 16 * (ii // 2 + 1))
                        tensor.wait_ge(sem["lruev"], ii)
                        tensor.matmul(pb[5][:], uBt[:, par * 128:(par + 1) * 128],
                                      xau[:, :], start=True, stop=True).then_inc(sem["umm"], 1)
                        if i >= 2:
                            ih = i - 2
                            tensor.wait_ge(sem["sH"], half * IC + ih + 1)
                            tensor.matmul(pb[6][0:32, :], dAb[:, ih * 32:(ih + 1) * 32],
                                          h_tile(ih), start=(ih == 0), stop=False)
                    for ih in (IC - 2, IC - 1):
                        tensor.wait_ge(sem["sH"], half * IC + ih + 1)
                        inst = tensor.matmul(pb[6][0:32, :], dAb[:, ih * 32:(ih + 1) * 32],
                                             h_tile(ih), start=False, stop=(ih == IC - 1))
                    inst.then_inc(sem["hdamm"], 1)

                    tensor.wait_ge(sem["hdaev"], half + 1)
                    tensor.wait_ge(sem["lrgev"], (half + 1) * IC)
                    tensor.wait_ge(sem["lruev"], (half + 1) * IC)
                    for r in range(NCORE):
                        if r >= 1:
                            tensor.wait_ge(sem["r2p"], half * NH + 4 * r)
                        for si, (i0, ssz, kch) in enumerate(DSLABS):
                            q = half * NCORE * nslab + r * nslab + si
                            qp = q % 2
                            tensor.wait_ge(sem["dld%d" % qp], 16 * (q // 2 + 1))
                            for c in range(ssz):
                                i = i0 + c
                                for ol in range(4):
                                    inst = tensor.matmul(pb[ol][:],
                                                         dtile[:, qp * 4 * HF + c * HF + ol * 128:
                                                               qp * 4 * HF + c * HF + ol * 128 + 128],
                                                         h_tile(i), start=(i == 0), stop=(i == IC - 1))
                                inst.then_inc(sem["dmm"], 1)
                        for ol in range(4):
                            oc = 4 * r + ol
                            kk2 = half * NH + oc
                            tensor.wait_ge(sem["dbl%d" % (kk2 % 2)], 16 * (kk2 // 2 + 1))
                            tensor.wait_ge(sem["lrdev"], kk2 - 1 if kk2 >= 2 else 0)
                            tensor.matmul(pb[4 + (kk2 % 2)][:], dBt[:, (kk2 % 2) * 128:(kk2 % 2 + 1) * 128],
                                          hdabuf[:, :], start=True, stop=True).then_inc(sem["dlr"], 1)
                        for ol in range(4):
                            oc = 4 * r + ol
                            kk2 = half * NH + oc
                            parp = kk2 % 2
                            first = (oc == 0)
                            tensor.wait_ge(sem["r2ev"], kk2 + 1)
                            tensor.matmul(pb[6][0:1, :], ones_col[:],
                                          r2stg[:, parp * HF:(parp + 1) * HF],
                                          start=first, stop=(oc == NH - 1))
                            tensor.wait_ge(sem["sqd"], kk2 + 1)
                            tensor.matmul(pb[7][0:1, :], ones_col[:],
                                          Sb[:, parp * HF:(parp + 1) * HF],
                                          start=first, stop=(oc == NH - 1)).then_inc(sem["dstats"], 1)
                    tensor.wait_ge(sem["a2"], 2 * (half + 1))
                    tensor.wait_ge(sem["lrdev"], (half + 1) * NH)
                    tensor.matmul(pb[4][:], ones_row[:], rows1[0:1, :], start=True, stop=True)
                    tensor.matmul(pb[5][:], ones_row[:], rows0[0:1, :],
                                  start=True, stop=True).then_inc(sem["bcC"], 1)

            @blk.scalar
            def _(scalar):
                for half in range(2):
                    for i in range(IC):
                        ii = half * IC + i
                        par = ii % 2
                        scalar.wait_ge(sem["gmm"], ii + 1)
                        if ii >= 2:
                            scalar.wait_ge(sem["sttG"], ii - 1)
                        scalar.activation(out=LRg[:, par * HF:(par + 1) * HF], in_=pb[4][:],
                                          func=AF.Copy).then_inc(sem["lrgev"], 1)
                        scalar.wait_ge(sem["umm"], ii + 1)
                        if ii >= 2:
                            scalar.wait_ge(sem["sttU"], ii - 1)
                        scalar.activation(out=LRu[:, par * HF:(par + 1) * HF], in_=pb[5][:],
                                          func=AF.Copy).then_inc(sem["lruev"], 1)
                        scalar.wait_ge(sem["sttG"], ii + 1)
                        if ii >= 2:
                            scalar.wait_ge(sem["sH"], ii - 1)
                        scalar.activation(out=Sb[:, par * HF:(par + 1) * HF],
                                          in_=Gb[:, par * HF:(par + 1) * HF],
                                          func=AF.Silu).then_inc(sem["sS"], 1)
                    for oc in range(NH):
                        kk2 = half * NH + oc
                        parp = kk2 % 2
                        scalar.wait_ge(sem["dlr"], kk2 + 1)
                        if kk2 >= 2:
                            scalar.wait_ge(sem["r2p"], kk2 - 1)
                        scalar.activation(out=LRu[:, parp * HF:(parp + 1) * HF],
                                          in_=pb[4 + (kk2 % 2)][:],
                                          func=AF.Copy).then_inc(sem["lrdev"], 1)
                        scalar.wait_ge(sem["r2ev"], kk2 + 1)
                        if kk2 >= 2:
                            scalar.wait_ge(sem["dstats"], kk2 - 1)
                        scalar.activation(out=Sb[:, parp * HF:(parp + 1) * HF],
                                          in_=r2stg[:, parp * HF:(parp + 1) * HF],
                                          func=AF.Square).then_inc(sem["sqd"], 1)
                    scalar.wait_ge(sem["v2"], 2 * (half + 1))
                    scalar.activation(out=rows2[0:1, :], in_=rows1[0:1, :],
                                      func=AF.Sqrt).then_inc(sem["sd2"], 1)

            @blk.vector
            def _(vector):
                ndv1c = [0]
                norm_steps = []
                for half in range(2):
                    vector.wait_ge(sem["xagmm"], half + 1)
                    if half == 1:
                        vector.wait_ge(sem["gmm"], IC)
                    vector.tensor_copy(out=xag[:, :], in_=pb[6][0:32, :]).then_inc(sem["xagev"], 1)
                    vector.wait_ge(sem["xaumm"], half + 1)
                    if half == 1:
                        vector.wait_ge(sem["umm"], IC)
                    vector.tensor_copy(out=xau[:, :], in_=pb[6][0:32, :]).then_inc(sem["xauev"], 1)
                    for i in range(IC):
                        ii = half * IC + i
                        par = ii % 2
                        vector.wait_ge(sem["gmm"], ii + 1)
                        vector.wait_ge(sem["lrgev"], ii + 1)
                        if ii >= 2:
                            vector.wait_ge(sem["sS"], ii - 1)
                        vector.scalar_tensor_tensor(out=Gb[:, par * HF:(par + 1) * HF],
                                                    in0=pb[par][:], scalar=s3g[:, i:i + 1],
                                                    in1=LRg[:, par * HF:(par + 1) * HF],
                                                    op0=ALU.mult, op1=ALU.add).then_inc(sem["sttG"], 1)
                        vector.wait_ge(sem["umm"], ii + 1)
                        vector.wait_ge(sem["lruev"], ii + 1)
                        if ii >= 2:
                            vector.wait_ge(sem["sH"], ii - 1)
                        vector.scalar_tensor_tensor(out=Ub[:, par * HF:(par + 1) * HF],
                                                    in0=pb[2 + par][:], scalar=s3u[:, i:i + 1],
                                                    in1=LRu[:, par * HF:(par + 1) * HF],
                                                    op0=ALU.mult, op1=ALU.add).then_inc(sem["sttU"], 1)
                        vector.wait_ge(sem["sS"], ii + 1)
                        vector.wait_ge(sem["sttU"], ii + 1)
                        if half == 1 and i == 0:
                            vector.wait_ge(sem["dmm"], NCORE * IC)
                        vector.tensor_tensor(out=h_tile(i),
                                             in0=Sb[:, par * HF:(par + 1) * HF],
                                             in1=Ub[:, par * HF:(par + 1) * HF],
                                             op=ALU.mult).then_inc(sem["sH"], 1)
                        if half == 1 and norm_steps:
                            norm_steps.pop(0)()
                    vector.wait_ge(sem["hdamm"], half + 1)
                    if half == 1:
                        vector.wait_ge(sem["dlr"], NH)
                    vector.tensor_copy(out=hdabuf[:, :], in_=pb[6][0:32, :]).then_inc(sem["hdaev"], 1)
                    vector.wait_ge(sem["sS"], (half + 1) * IC)
                    for r in range(NCORE):
                        for ol in range(4):
                            oc = 4 * r + ol
                            kk2 = half * NH + oc
                            parp = kk2 % 2
                            vector.wait_ge(sem["dmm"], half * NCORE * IC + (r + 1) * IC)
                            vector.wait_ge(sem["lrdev"], kk2 + 1)
                            vector.scalar_tensor_tensor(out=Gb[:, parp * HF:(parp + 1) * HF],
                                                        in0=pb[ol][:], scalar=s3d[:, oc:oc + 1],
                                                        in1=LRu[:, parp * HF:(parp + 1) * HF],
                                                        op0=ALU.mult, op1=ALU.add).then_inc(sem["r2p"], 1)
                            vector.wait_ge(sem["r2p"], kk2 + 1)
                            if kk2 >= 2:
                                vector.wait_ge(sem["sqd"], kk2 - 1)
                                vector.wait_ge(sem["dstats"], kk2 - 1)
                                vector.wait_ge(sem["r2st%d" % (kk2 % 2)], 16 * (kk2 // 2))
                            vector.tensor_tensor(out=r2stg[:, parp * HF:(parp + 1) * HF],
                                                 in0=Gb[:, parp * HF:(parp + 1) * HF],
                                                 in1=x2sb_t(oc), op=ALU.add).then_inc(sem["r2ev"], 1)
                    vector.wait_ge(sem["dstats"], (half + 1) * NH)
                    vector.tensor_scalar(out=rows0[0:1, :], in0=pb[6][0:1, :],
                                         scalar1=1.0 / D, scalar2=None, op0=ALU.mult)
                    vector.tensor_scalar(out=rows1[0:1, :], in0=pb[7][0:1, :],
                                         scalar1=1.0 / D, scalar2=None, op0=ALU.mult).then_inc(sem["m2"], 1)
                    vector.wait_ge(sem["m2"], half + 1)
                    vector.tensor_tensor(out=rows2[0:1, :], in0=rows0[0:1, :],
                                         in1=rows0[0:1, :], op=ALU.mult).then_inc(sem["v2"], 1)
                    vector.wait_ge(sem["v2"], 2 * half + 1)
                    vector.scalar_tensor_tensor(out=rows1[0:1, :], in0=rows1[0:1, :],
                                                scalar=1e-5, in1=rows2[0:1, :],
                                                op0=ALU.add, op1=ALU.subtract).then_inc(sem["v2"], 1)
                    vector.wait_ge(sem["sd2"], half + 1)
                    vector.reciprocal(out=rows1[0:1, :], in_=rows2[0:1, :]).then_inc(sem["a2"], 1)
                    vector.wait_ge(sem["a2"], 2 * half + 1)
                    vector.scalar_tensor_tensor(out=rows0[0:1, :], in0=rows0[0:1, :],
                                                scalar=-1.0, in1=rows1[0:1, :],
                                                op0=ALU.mult, op1=ALU.mult).then_inc(sem["a2"], 1)
                    vector.wait_ge(sem["bcC"], half + 1)
                    vector.wait_ge(sem["r2st0"], 16 * (NH // 2) * (half + 1))
                    vector.wait_ge(sem["r2st1"], 16 * (NH // 2) * (half + 1))
                    vector.tensor_copy(out=r2stg[:, 0:HF], in_=pb[4][:])
                    vector.tensor_copy(out=r2stg[:, HF:2 * HF], in_=pb[5][:]).then_inc(sem["bccp"], 1)
                    vector.wait_ge(sem["bccp"], half + 1)
                    def mk_norm(half_, oc):
                        def f():
                            kk2 = half_ * NH + oc
                            parp = kk2 % 2
                            vector.wait_ge(sem["r2ld%d" % (kk2 % 2)], 16 * (kk2 // 2 + 1))
                            if kk2 >= 2:
                                vector.wait_ge(sem["outst%d" % (kk2 % 2)], 16 * (kk2 // 2))
                            ndv1c[0] += 1
                            vector.tensor_tensor(out=outstg[:, parp * HF:(parp + 1) * HF],
                                                 in0=outstg[:, parp * HF:(parp + 1) * HF],
                                                 in1=r2stg[:, 0:HF], op=ALU.mult).then_inc(sem["dv1"], 1)
                            vector.wait_ge(sem["dv1"], 64 + ndv1c[0])
                            vector.tensor_tensor(out=outstg[:, parp * HF:(parp + 1) * HF],
                                                 in0=outstg[:, parp * HF:(parp + 1) * HF],
                                                 in1=r2stg[:, HF:2 * HF], op=ALU.add).then_inc(sem["outn"], 1)
                        return f
                    if half == 0:
                        for oc in range(NH):
                            norm_steps.append(mk_norm(0, oc))
                    else:
                        for oc in range(NH):
                            mk_norm(1, oc)()

            @blk.gpsimd
            def _(g):
                ofv = of.rearrange("(c p) n -> c p n", p=128)
                for half in range(2):
                    for oc in range(NH):
                        kk2 = half * NH + oc
                        parp = kk2 % 2
                        g.wait_ge(sem["r2ev"], kk2 + 1)
                        g.dma_start(out=r2v[oc, :, half * HF:(half + 1) * HF],
                                    in_=r2stg[:, parp * HF:(parp + 1) * HF]).then_inc(sem["r2st%d" % (kk2 % 2)], 16)
                    for oc in range(NH):
                        kk2 = half * NH + oc
                        parp = kk2 % 2
                        if oc == 0:
                            g.wait_ge(sem["r2st0"], 16 * (NH // 2) * (half + 1))
                            g.wait_ge(sem["r2st1"], 16 * (NH // 2) * (half + 1))
                        if kk2 >= 2:
                            g.wait_ge(sem["outn"], kk2 - 1)
                            g.wait_ge(sem["outst%d" % (kk2 % 2)], 16 * (kk2 // 2))
                        g.dma_start(out=outstg[:, parp * HF:(parp + 1) * HF],
                                    in_=r2v[oc, :, half * HF:(half + 1) * HF]).then_inc(sem["r2ld%d" % (kk2 % 2)], 16)
                        g.wait_ge(sem["outn"], kk2 + 1)
                        g.dma_start(out=ofv[oc][:, half * HF:(half + 1) * HF],
                                    in_=outstg[:, parp * HF:(parp + 1) * HF]).then_inc(sem["outst%d" % (kk2 % 2)], 16)
                g.wait_ge(sem["outst0"], 16 * NH)
                g.wait_ge(sem["outst1"], 16 * NH)

        cstack.close()
    pstack.close()
    stack.close()
    nc.finalize()
    return nc


def _prep_inputs(inputs):
    A = {k: np.asarray(v) for k, v in inputs.items()}
    x = A["L0_attn_x"].astype(np.float32, copy=False)
    w1 = A["L0_attn_w1"]; w2 = A["L0_attn_w2"]; b2 = A["L0_attn_b2"]
    mask = A["L0_attn_sparse_mask"]
    x_fm = np.ascontiguousarray(x.reshape(NTOK_, D).T)
    w2T = np.ascontiguousarray(w2.T)
    b2hh = np.ascontiguousarray(b2.reshape(NH, 128).T)

    gq = A["L0_ffn_gate_q"]; uq = A["L0_ffn_up_q"]; dq = A["L0_ffn_down_q"]
    gqb = gq.T.reshape(NH, 128, IC, 128)
    uqb = uq.T.reshape(NH, 128, IC, 128)
    dqb = dq.T.reshape(IC, 128, NH, 128)
    gsch = np.ascontiguousarray(A["L0_ffn_gate_scale"].reshape(IC, 128).T)
    usch = np.ascontiguousarray(A["L0_ffn_up_scale"].reshape(IC, 128).T)
    dsch = np.ascontiguousarray(A["L0_ffn_down_scale"].reshape(NH, 128).T)
    gAhh = np.ascontiguousarray(
        A["L0_ffn_gate_A"].reshape(NH, 128, 32).transpose(1, 0, 2)).reshape(128, NH * 32)
    uAhh = np.ascontiguousarray(
        A["L0_ffn_up_A"].reshape(NH, 128, 32).transpose(1, 0, 2)).reshape(128, NH * 32)
    dAhh = np.ascontiguousarray(
        A["L0_ffn_down_A"].reshape(IC, 128, 32).transpose(1, 0, 2)).reshape(128, IC * 32)
    gBhh = np.ascontiguousarray(A["L0_ffn_gate_B"])
    uBhh = np.ascontiguousarray(A["L0_ffn_up_B"])
    dBhh = np.ascontiguousarray(A["L0_ffn_down_B"])

    in_maps = []
    for c in range(NCORE):
        w1uc = np.ascontiguousarray(
            w1[:, c * 512:(c + 1) * 512].reshape(NH, 128, 512))
        lo = 4 * c - 2
        w2uc = np.zeros((D, 1024), dtype=np.float32)
        blo, bhi = max(0, lo), min(NH, 4 * c + 6)
        w2uc[:, (blo - lo) * 128:(bhi - lo) * 128] = w2T[:, blo * 128:bhi * 128]
        w2uc = np.ascontiguousarray(w2uc.reshape(NH, 128, 1024))
        mskc = np.zeros((128, 4 * SW), dtype=np.float32)
        for j in range(4):
            bs = 4 * c + j
            slo = bs - 2
            sblo, sbhi = max(0, slo), min(NH, bs + 3)
            mskc[:, j * SW + (sblo - slo) * 128: j * SW + (sbhi - slo) * 128] = \
                mask[bs * 128:(bs + 1) * 128, sblo * 128:sbhi * 128]
        in_maps.append({
            "xi": np.ascontiguousarray(x_fm[:, c * TPC:(c + 1) * TPC]),
            "w1u": w1uc,
            "w2tu": w2uc,
            "msku": mskc,
            "b2h": b2hh,
            "gqh": np.ascontiguousarray(gqb[4 * c:4 * c + 4].transpose(1, 2, 0, 3)).reshape(128, IC * 512),
            "uqh": np.ascontiguousarray(uqb[4 * c:4 * c + 4].transpose(1, 2, 0, 3)).reshape(128, IC * 512),
            "dqh": np.ascontiguousarray(dqb.transpose(1, 0, 2, 3)[:, :, 4 * c:4 * c + 4, :]).reshape(128, IC * 512),
            "gsc": gsch, "usc": usch, "dsc": dsch,
            "gAh": gAhh, "uAh": uAhh, "dAh": dAhh,
            "gBh": gBhh, "uBh": uBhh, "dBh": dBhh,
        })
    return in_maps


NTOK_ = 8192


def kernel(**inputs):
    in_maps = _prep_inputs(inputs)
    nc = build_program()
    res = run_bass_kernel_spmd(nc, in_maps, core_ids=list(range(NCORE)))
    out_fm = np.concatenate([res.results[c]["of"] for c in range(NCORE)], axis=1)
    return np.ascontiguousarray(out_fm.T).reshape(2, D, D).astype(np.float32)
